# version 05
# baseline (speedup 1.0000x reference)
import sys
sys.path.insert(0, "/opt/trn_rl_repo")
import math
import numpy as np
import ml_dtypes
import concourse.bass as bass
import concourse.mybir as mybir
from concourse.bass_utils import run_bass_kernel_spmd

F32 = mybir.dt.float32
BF16 = mybir.dt.bfloat16
I32 = mybir.dt.int32
AX = mybir.AxisListType.X
ALU = mybir.AluOpType
ACT = mybir.ActivationFunctionType

NCORES = 8
T = 2048
D = 1024
NT = T // 128
NC = D // 128
LAMBDA_INIT = 0.8 - 0.6 * math.exp(-0.3 * 12)
SCALE_A = 0.125
SCALE_B = 1.0 / math.sqrt(32.0)
EPS_LN = 1e-5
EPS_RMS = 1e-6


def _host_consts():
    i = np.arange(128)
    inv_a = 10000.0 ** (-2.0 * np.arange(32, dtype=np.float64) / 64.0)
    t = np.arange(T, dtype=np.float64)
    hd_a = (i % 64) // 2
    ang_a = np.outer(inv_a[hd_a], t)
    CLa = np.cos(ang_a)
    sgn_a = np.where(i % 2 == 0, -1.0, 1.0)
    SLa = np.sin(ang_a) * sgn_a[:, None]
    inv_b = 10000.0 ** (-2.0 * np.arange(16, dtype=np.float64) / 32.0)
    ang_b = np.outer(inv_b[i % 16], t)
    CLb = np.cos(ang_b)
    sgn_b = np.where((i % 32) < 16, 1.0, -1.0)
    SLb = np.sin(ang_b) * sgn_b[:, None]
    ID = np.eye(128, dtype=np.float32)
    maskT = np.where(np.arange(128)[None, :] > np.arange(128)[:, None],
                     -1000.0, 0.0).astype(np.float32)
    return (CLa.astype(np.float32), SLa.astype(np.float32),
            CLb.astype(np.float32), SLb.astype(np.float32), ID, maskT)


def build():
    nc = bass.Bass("TRN2", target_bir_lowering=False, debug=False)
    d_in = {}
    d_in["xT"] = nc.dram_tensor("xT", [D, T], F32, kind="ExternalInput").ap()
    for n, shp in [("WQT", [D, D]), ("WKT", [D, D]), ("WVT", [D, D]), ("WOT", [D, D]),
                   ("Wq", [D, D]), ("Wk", [D, 512]), ("Wv", [D, 512]), ("Wo", [D, D]),
                   ("FWT", [D, D])]:
        d_in[n] = nc.dram_tensor(n, shp, F32, kind="ExternalInput").ap()
    for n, shp in [("CLa", [128, T]), ("SLa", [128, T]), ("CLb", [128, T]), ("SLb", [128, T]),
                   ("IDC", [128, 128]), ("MSK", [128, 128])]:
        d_in[n] = nc.dram_tensor(n, shp, F32, kind="ExternalInput").ap()
    for n, shp in [("lq1", [1, 32]), ("lk1", [1, 32]), ("lq2", [1, 32]), ("lk2", [1, 32]),
                   ("rmsw", [1, 64]), ("gam", [D]), ("bet", [D])]:
        d_in[n] = nc.dram_tensor(n, shp if isinstance(shp[0], int) and len(shp) > 1 else [1, shp[0]], F32, kind="ExternalInput").ap()
    out_d = nc.dram_tensor("outT", [D, T], F32, kind="ExternalOutput").ap()
    wscr = nc.dram_tensor("wscr", [D, D], BF16)
    cc_in = nc.dram_tensor("cc_in", [1, 8], F32)
    cc_out = nc.dram_tensor("cc_out", [1, 8], F32)

    from contextlib import ExitStack
    es = ExitStack()
    with es:
        sb = lambda n, shp, dt=BF16: es.enter_context(nc.sbuf_tensor(n, shp, dt))
        XT = sb("XT", [128, NC * T], F32)
        FS = sb("FS", [128, T], F32)
        ROWS = sb("ROWS", [1, 4096], F32)
        AXR = sb("AXR", [1, 400], F32)
        AUX = sb("AUX", [128, 64], F32)
        XLN = sb("XLN", [128, NC * T])
        CTXT = sb("CTXT", [128, NC * T])
        TABC = sb("TABC", [128, T])
        TABS = sb("TABS", [128, T])
        QB = sb("QB", [128, T])
        KB = sb("KB", [128, 2 * T])
        VB = sb("VB", [128, NT * 256])
        PT = sb("PT", [128, 2 * 1024])
        SQB = sb("SQB", [128, T])
        RSC = sb("RSC", [128, T])
        WA = sb("WA", [128, 4096])
        ONE2 = sb("ONE2", [128, 2])
        ONE4 = sb("ONE4", [128, 4])
        ONEB = sb("ONEB", [128, 1])
        ONER = sb("ONER", [1, 128], F32)
        CST = sb("CST", [128, 16], F32)
        IDB = sb("IDB", [128, 128])
        IDF = sb("IDF", [128, 128], F32)
        MSKB = sb("MSKB", [128, 128])
        ICST = sb("ICST", [128, 512], I32)
        CCS = sb("CCS", [1, 8], F32)
        psA = es.enter_context(nc.psum_tensor("psA", [128, 1024], F32))
        psB = es.enter_context(nc.psum_tensor("psB", [128, 1024], F32))
        psCa = es.enter_context(nc.psum_tensor("psCa", [128, 1024], F32))
        psCb = es.enter_context(nc.psum_tensor("psCb", [128, 1024], F32))
        SEM = {}
        for s in ("PE", "ACTT", "DVE", "LD", "LDW", "ST", "GP", "CC"):
            SEM[s] = es.enter_context(nc.semaphore(s))

        _cvals = [EPS_LN, EPS_RMS, -0.5, -1.0, 1.0, 1.0 / 64.0, SCALE_A, SCALE_B, 0.0]
        for _i, _v in enumerate(_cvals):
            nc.const_aps.register(F32, _v, CST[:, _i:_i + 1])

        Q = {k: [] for k in ("sp", "pe", "act", "dve", "gp")}
        N = {k: 0 for k in SEM}
        last_wait = {(e, s): 0 for e in Q for s in SEM}

        def put(eng, fn):
            Q[eng].append(fn)

        def tag(eng, fn, sem, by=1):
            N[sem] += by
            tok = N[sem]
            put(eng, lambda e, fn=fn, s=sem, b=by: fn(e).then_inc(SEM[s], b))
            return tok

        def wait(eng, sem, tok):
            if tok is None or tok <= 0:
                return
            if last_wait[(eng, sem)] >= tok:
                return
            last_wait[(eng, sem)] = tok
            put(eng, lambda e, s=sem, t=tok: e.wait_ge(SEM[s], t))

        def seg512(a, b):
            out = []
            while a < b:
                nb = min(b, (a // 512 + 1) * 512)
                out.append((a, nb))
                a = nb
            return out

        ldx = None
        for c in range(NC):
            ldx = tag("sp", lambda e, c=c: e.dma_start(
                out=XT[:, c * T:(c + 1) * T], in_=d_in["xT"][c * 128:(c + 1) * 128, :]),
                "LD", 16)
        ldc = None
        for nm, dst in [("CLa", TABC), ("SLa", TABS)]:
            ldc = tag("gp", lambda e, nm=nm, dst=dst: e.dma_start(out=dst[:, :], in_=d_in[nm]), "GP", 16)
        for nm, dst in [("IDC", IDB), ("MSK", MSKB)]:
            ldc = tag("gp", lambda e, nm=nm, dst=dst: e.dma_start(out=dst[:, :], in_=d_in[nm]), "GP", 16)
        ldf = tag("sp", lambda e: e.dma_start(out=IDF[:, :], in_=d_in["IDC"]), "LD", 16)
        ldg = tag("sp", lambda e: e.dma_start(
            out=AUX[:, 0:NC], in_=d_in["gam"].rearrange("o (c p) -> (o p) c", p=128),
            allow_slow_non_contiguous=True), "LD", 16)
        ldg = tag("sp", lambda e: e.dma_start(
            out=AUX[:, 8:8 + NC], in_=d_in["bet"].rearrange("o (c p) -> (o p) c", p=128),
            allow_slow_non_contiguous=True), "LD", 16)
        ldl = None
        for i, nm in enumerate(["lq1", "lk1", "lq2", "lk2"]):
            ldl = tag("sp", lambda e, i=i, nm=nm: e.dma_start(out=ROWS[0:1, 3072 + 32 * i:3104 + 32 * i], in_=d_in[nm]), "LD", 16)
        ldl = tag("sp", lambda e: e.dma_start(out=AXR[0:1, 64:128], in_=d_in["rmsw"]), "LD", 16)

        def dve_chain(fn):
            wait("dve", "DVE", N["DVE"])
            return tag("dve", fn, "DVE")
        for _i, _v in enumerate(_cvals):
            dve_chain(lambda e, _i=_i, _v=_v: e.memset(CST[:, _i:_i + 1], _v))
        dve_chain(lambda e: e.memset(ONER[:, :], 1.0))
        dve_chain(lambda e: e.memset(ONEB[:, :], 1.0))
        dve_chain(lambda e: e.memset(ONE2[:, :], 0.0))
        dve_chain(lambda e: e.memset(ONE2[0:64, 0:1], 1.0))
        dve_chain(lambda e: e.memset(ONE2[64:128, 1:2], 1.0))
        dve_chain(lambda e: e.memset(ONE4[:, :], 0.0))
        for j in range(4):
            dve_chain(lambda e, j=j: e.memset(ONE4[32 * j:32 * j + 32, j:j + 1], 1.0))
        dve_chain(lambda e: e.memset(AXR[0:1, 220:228], 0.0))
        vbones = dve_chain(lambda e: e.memset(
            VB[:, :].rearrange("p (b k) -> p b k", k=128)[:, :, 64:128], 1.0))

        wait("dve", "LD", ldl)
        t1 = tag("dve", lambda e: e.tensor_tensor(AXR[0:1, 128:160], ROWS[0:1, 3072:3104], ROWS[0:1, 3104:3136], op=ALU.mult), "DVE")
        t2 = tag("dve", lambda e: e.tensor_tensor(AXR[0:1, 160:192], ROWS[0:1, 3136:3168], ROWS[0:1, 3168:3200], op=ALU.mult), "DVE")
        wait("dve", "DVE", t2)
        t3 = tag("dve", lambda e: e.reduce_sum(AXR[0:1, 192:193], AXR[0:1, 128:160], axis=AX), "DVE")
        t4 = tag("dve", lambda e: e.reduce_sum(AXR[0:1, 193:194], AXR[0:1, 160:192], axis=AX), "DVE")
        wait("act", "DVE", t4)
        t5 = tag("act", lambda e: e.activation(AXR[0:1, 194:195], AXR[0:1, 192:193], ACT.Exp), "ACTT")
        t6 = tag("act", lambda e: e.activation(AXR[0:1, 195:196], AXR[0:1, 193:194], ACT.Exp), "ACTT")
        wait("dve", "ACTT", t6)
        t7 = tag("dve", lambda e: e.tensor_sub(AXR[0:1, 196:197], AXR[0:1, 194:195], AXR[0:1, 195:196]), "DVE")
        wait("dve", "DVE", t7)
        t8 = tag("dve", lambda e: e.tensor_scalar(AXR[0:1, 197:198], AXR[0:1, 196:197], 1.0, -LAMBDA_INIT, op0=ALU.mult, op1=ALU.subtract), "DVE")
        wait("dve", "DVE", t8)
        t8b = tag("dve", lambda e: e.tensor_scalar_mul(AXR[0:1, 197:198], AXR[0:1, 197:198], -1.0), "DVE")
        wait("dve", "DVE", t8b)
        t9 = tag("dve", lambda e: e.tensor_scalar_add(AXR[0:1, 198:199], AXR[0:1, 196:197], LAMBDA_INIT), "DVE")
        wait("dve", "DVE", t9)
        t10 = tag("dve", lambda e: e.tensor_scalar_mul(AXR[0:1, 199:200], AXR[0:1, 198:199], -1.0), "DVE")
        t11 = tag("dve", lambda e: e.tensor_scalar_mul(AXR[0:1, 256:320], AXR[0:1, 64:128], 1.0 - LAMBDA_INIT), "DVE")
        wait("pe", "DVE", t10)
        p1 = tag("pe", lambda e: e.matmul(psA[:, 0:1], ONER[0:1, :], AXR[0:1, 199:200], start=True, stop=True), "PE")
        wait("dve", "PE", p1)
        t12 = tag("dve", lambda e: e.tensor_copy(AUX[:, 38:39], psA[:, 0:1]), "DVE")

        put("dve", lambda e: e.memset(AUX[:, 41:44], 0.0))
        wsum_tok = None
        ld_tok = {}
        for c in range(NC):
            wait("sp", "DVE", N["DVE"])
            ld_tok[c] = tag("sp", lambda e, c=c: e.dma_start(
                out=FS[:, 0:1024],
                in_=d_in["FWT"][c * 128:(c + 1) * 128, :]), "LDW", 16)
            wait("dve", "LDW", ld_tok[c])
            wait("act", "LDW", ld_tok[c])
            wait("act", "DVE", N["DVE"])
            a1 = tag("act", lambda e: e.activation(FS[:, 1024:2048], FS[:, 0:1024], ACT.Abs), "ACTT")
            wait("dve", "ACTT", a1)
            a2 = tag("dve", lambda e: e.reduce_sum(AUX[:, 45:46], FS[:, 1024:2048], axis=AX), "DVE")
            wait("dve", "DVE", a2)
            wsum_tok = tag("dve", lambda e: e.tensor_add(AUX[:, 41:42], AUX[:, 41:42], AUX[:, 45:46]), "DVE")
            wait("dve", "DVE", wsum_tok)
        wait("pe", "DVE", wsum_tok)
        pt = tag("pe", lambda e: e.transpose(psA[0:1, 0:128], AUX[:, 41:42], IDF[:, :]), "PE")
        wait("dve", "PE", pt)
        dsum = tag("dve", lambda e: e.reduce_sum(AXR[0:1, 200:201], psA[0:1, 0:128], axis=AX), "DVE")
        wait("dve", "DVE", dsum)
        ddel = tag("dve", lambda e: e.tensor_scalar_mul(AXR[0:1, 201:202], AXR[0:1, 200:201], 0.7 / (D * D)), "DVE")
        wait("pe", "DVE", ddel)
        pdel = tag("pe", lambda e: e.matmul(psA[:, 0:1], ONER[0:1, :], AXR[0:1, 201:202], start=True, stop=True), "PE")
        wait("dve", "PE", pdel)
        dcol = tag("dve", lambda e: e.tensor_copy(AUX[:, 46:47], psA[:, 0:1]), "DVE")
        wq_gp = None
        for c in range(NC):
            wait("sp", "DVE", N["DVE"])
            wait("sp", "ACTT", N["ACTT"])
            l2 = tag("sp", lambda e, c=c: e.dma_start(out=FS[:, 0:1024], in_=d_in["FWT"][c * 128:(c + 1) * 128, :]), "LDW", 16)
            wait("dve", "LDW", l2)
            wait("dve", "DVE", dcol)
            if wq_gp is not None:
                wait("dve", "GP", wq_gp)
            wait("act", "LDW", l2)
            wait("act", "DVE", N["DVE"])
            b1 = tag("act", lambda e: e.activation(FS[:, 1024:2048], FS[:, 0:1024], ACT.Abs), "ACTT")
            wait("dve", "ACTT", b1)
            b2 = tag("dve", lambda e: e.tensor_single_scalar(RSC[:, 0:1024], FS[:, 1024:2048], AUX[:, 46:47], op=ALU.is_gt), "DVE")
            wait("act", "LDW", l2)
            wait("act", "GP", wq_gp if wq_gp else 0)
            b3 = tag("act", lambda e: e.activation(SQB[:, 0:1024], FS[:, 0:1024], ACT.Sign), "ACTT")
            wait("dve", "ACTT", b3)
            wait("dve", "DVE", b2)
            b4 = tag("dve", lambda e: e.tensor_tensor(XLN[:, 0:1024], RSC[:, 0:1024], SQB[:, 0:1024], op=ALU.mult), "DVE")
            b5 = tag("dve", lambda e: e.tensor_tensor(FS[:, 1024:2048], FS[:, 1024:2048], RSC[:, 0:1024], op=ALU.mult), "DVE")
            wait("dve", "DVE", b5)
            b6 = tag("dve", lambda e: e.reduce_sum(AUX[:, 45:46], FS[:, 1024:2048], axis=AX), "DVE")
            wait("dve", "DVE", b6)
            b7 = tag("dve", lambda e: e.tensor_add(AUX[:, 42:43], AUX[:, 42:43], AUX[:, 45:46]), "DVE")
            b8 = tag("dve", lambda e: e.reduce_sum(AUX[:, 47:48], RSC[:, 0:1024], axis=AX), "DVE")
            wait("dve", "DVE", b8)
            b9 = tag("dve", lambda e: e.tensor_add(AUX[:, 43:44], AUX[:, 43:44], AUX[:, 47:48]), "DVE")
            wait("dve", "DVE", b9)
            wait("gp", "DVE", b4)
            wq_gp = tag("gp", lambda e, c=c: e.dma_start(
                out=wscr.ap()[c * 128:(c + 1) * 128, :], in_=XLN[:, 0:1024]), "GP", 16)
        wait("pe", "DVE", b9)
        pa = tag("pe", lambda e: e.transpose(psA[0:1, 0:128], AUX[:, 42:43], IDF[:, :]), "PE")
        pb = tag("pe", lambda e: e.transpose(psB[0:1, 0:128], AUX[:, 43:44], IDF[:, :]), "PE")
        wait("dve", "PE", pb)
        c1 = tag("dve", lambda e: e.reduce_sum(AXR[0:1, 202:203], psA[0:1, 0:128], axis=AX), "DVE")
        c2 = tag("dve", lambda e: e.reduce_sum(AXR[0:1, 203:204], psB[0:1, 0:128], axis=AX), "DVE")
        wait("dve", "DVE", c2)
        c3 = tag("dve", lambda e: e.tensor_scalar_max(AXR[0:1, 203:204], AXR[0:1, 203:204], 1.0), "DVE")
        wait("dve", "DVE", c3)
        c4 = tag("dve", lambda e: e.reciprocal(AXR[0:1, 204:205], AXR[0:1, 203:204]), "DVE")
        wait("dve", "DVE", c4)
        alpha_tok = tag("dve", lambda e: e.tensor_tensor(AXR[0:1, 205:206], AXR[0:1, 202:203], AXR[0:1, 204:205], op=ALU.mult), "DVE")

        def emit_ln(dep_dve=None, dep_pe=None):
            wait("act", "DVE", N["DVE"]); wait("act", "PE", N["PE"]); wait("act", "LD", N["LD"])
            wait("dve", "PE", N["PE"]); wait("dve", "ACTT", N["ACTT"]); wait("dve", "GP", N["GP"])
            wait("pe", "DVE", N["DVE"]); wait("pe", "ACTT", N["ACTT"])
            pe_tok = None
            for c in range(NC):
                if pe_tok is not None:
                    wait("act", "PE", pe_tok)
                    wait("dve", "PE", pe_tok)
                a = tag("act", lambda e, c=c: e.copy(SQB[:, :], XT[:, c * T:(c + 1) * T]), "ACTT")
                wait("dve", "ACTT", a)
                sq = tag("dve", lambda e: e.tensor_tensor(RSC[:, :], SQB[:, :], SQB[:, :], op=ALU.mult), "DVE")
                wait("pe", "ACTT", a)
                wait("pe", "DVE", sq)
                for hf in range(2):
                    ps = psA if hf == 0 else psB
                    ps2 = psCa if hf == 0 else psCb
                    for sgi, (s0, s1) in enumerate(seg512(0, 1024)):
                        pe_tok = tag("pe", lambda e, ps=ps, hf=hf, s0=s0, s1=s1, c=c: e.matmul(
                            ps[0:1, s0:s1], ONEB[:, 0:1], SQB[:, hf * 1024 + s0:hf * 1024 + s1],
                            start=(c == 0), stop=(c == NC - 1)), "PE")
                        pe_tok = tag("pe", lambda e, ps2=ps2, hf=hf, s0=s0, s1=s1, c=c: e.matmul(
                            ps2[0:1, s0:s1], ONEB[:, 0:1], RSC[:, hf * 1024 + s0:hf * 1024 + s1],
                            start=(c == 0), stop=(c == NC - 1)), "PE")
            wait("dve", "PE", pe_tok)
            r = None
            for hf in range(2):
                ps = psA if hf == 0 else psB
                ps2 = psCa if hf == 0 else psCb
                r = tag("dve", lambda e, ps=ps, hf=hf: e.tensor_scalar_mul(ROWS[0:1, 2048 + hf * 1024:2048 + hf * 1024 + 1024], ps[0:1, :], 1.0 / D), "DVE")
                wait("dve", "DVE", r)
                r = tag("dve", lambda e, hf=hf: e.tensor_tensor(FS[0:1, hf * 1024:hf * 1024 + 1024], ROWS[0:1, 2048 + hf * 1024:2048 + hf * 1024 + 1024], ROWS[0:1, 2048 + hf * 1024:2048 + hf * 1024 + 1024], op=ALU.mult), "DVE")
                wait("dve", "DVE", r)
                r = tag("dve", lambda e, ps2=ps2, hf=hf: e.scalar_tensor_tensor(FS[0:1, hf * 1024:hf * 1024 + 1024], ps2[0:1, :], 1.0 / D, FS[0:1, hf * 1024:hf * 1024 + 1024], op0=ALU.mult, op1=ALU.subtract), "DVE")
            wait("act", "DVE", r)
            r = tag("act", lambda e: e.activation(FS[0:1, 0:T], FS[0:1, 0:T], ACT.Ln, bias=EPS_LN, scale=1.0), "ACTT")
            wait("act", "ACTT", r)
            r = tag("act", lambda e: e.activation(ROWS[0:1, 0:T], FS[0:1, 0:T], ACT.Exp, scale=-0.5), "ACTT")
            wait("dve", "ACTT", r)
            r2 = tag("dve", lambda e: e.tensor_tensor(ROWS[0:1, 2048:2048 + T], ROWS[0:1, 2048:2048 + T], ROWS[0:1, 0:T], op=ALU.mult), "DVE")
            wait("pe", "ACTT", r)
            wait("pe", "DVE", r2)
            ptok = None
            for hf in range(2):
                ps1 = psA if hf == 0 else psB
                ps2 = psCa if hf == 0 else psCb
                for (s0, s1) in seg512(0, 1024):
                    ptok = tag("pe", lambda e, ps1=ps1, hf=hf, s0=s0, s1=s1: e.matmul(
                        ps1[:, s0:s1], ONER[0:1, :], ROWS[0:1, hf * 1024 + s0:hf * 1024 + s1],
                        start=True, stop=True), "PE")
                    ptok = tag("pe", lambda e, ps2=ps2, hf=hf, s0=s0, s1=s1: e.matmul(
                        ps2[:, s0:s1], ONER[0:1, :], ROWS[0:1, 2048 + hf * 1024 + s0:2048 + hf * 1024 + s1],
                        start=True, stop=True), "PE")
            wait("dve", "PE", ptok)
            dtok = None
            for c in range(NC):
                for hf in range(2):
                    ps1 = psA if hf == 0 else psB
                    ps2 = psCa if hf == 0 else psCb
                    u = tag("dve", lambda e, c=c, hf=hf, ps1=ps1: e.tensor_tensor(
                        FS[:, hf * 1024:hf * 1024 + 1024], XT[:, c * T + hf * 1024:c * T + hf * 1024 + 1024],
                        ps1[:, :], op=ALU.mult), "DVE")
                    wait("dve", "DVE", u)
                    dtok = tag("dve", lambda e, c=c, hf=hf, ps2=ps2: e.tensor_tensor(
                        XLN[:, c * T + hf * 1024:c * T + hf * 1024 + 1024],
                        FS[:, hf * 1024:hf * 1024 + 1024], ps2[:, :], op=ALU.subtract), "DVE")
            return dtok

        def emit_rope(src_ps, rows, dst, dst_rows, shift, half_tok, s1=None, s2=None):
            r0, r1 = rows
            d0, d1 = dst_rows
            wait("act", "DVE", N["DVE"])
            wait("act", "PE", N["PE"])
            wait("dve", "PE", N["PE"])
            wait("dve", "GP", N["GP"])
            sA_, sB_ = (s1 if s1 is not None else psA), (s2 if s2 is not None else psB)
            a1 = tag("act", lambda e, sA_=sA_: e.copy(RSC[r0:r1, 0:1024], sA_[r0:r1, :]), "ACTT")
            a2 = tag("act", lambda e, sB_=sB_: e.copy(RSC[r0:r1, 1024:2048], sB_[r0:r1, :]), "ACTT")
            wait("dve", "ACTT", a2)
            mask = [(i ^ 1) for i in range(32)] if shift == 1 else [(i + 16) % 32 for i in range(32)]
            g2 = tag("dve", lambda e, mask=mask: e.stream_shuffle(SQB[r0:r1, :], RSC[r0:r1, :], mask), "DVE")
            wait("dve", "ACTT", a2)
            wait("dve", "DVE", g2)
            v1 = tag("dve", lambda e: e.tensor_tensor(RSC[r0:r1, :],
                                                      RSC[r0:r1, :], TABC[r0:r1, :], op=ALU.mult), "DVE")
            wait("dve", "DVE", v1)
            v2 = tag("dve", lambda e: e.tensor_tensor(SQB[r0:r1, :], SQB[r0:r1, :], TABS[r0:r1, :], op=ALU.mult), "DVE")
            wait("dve", "DVE", v2)
            v3 = tag("dve", lambda e: e.tensor_tensor(dst[d0:d1, 0:T], RSC[r0:r1, :], SQB[r0:r1, :], op=ALU.add), "DVE")
            return v3

        def emit_norm_bound(qsrc, qrows_masks, ksrc, krows_masks, scale, aux_cols, ones_lhs):
            nheads = len(aux_cols)
            wait("dve", "PE", N["PE"]); wait("dve", "ACTT", N["ACTT"])
            wait("pe", "ACTT", N["ACTT"])
            sq = tag("dve", lambda e: e.tensor_tensor(RSC[:, :], qsrc[:, 0:T], qsrc[:, 0:T], op=ALU.mult), "DVE")
            wait("pe", "DVE", sq)
            ptok = None
            for hf in range(2):
                ps = psA if hf == 0 else psB
                for (s0, s1) in seg512(0, 1024):
                    ptok = tag("pe", lambda e, ps=ps, hf=hf, s0=s0, s1=s1: e.matmul(
                        ps[0:nheads, s0:s1], ones_lhs, RSC[:, hf * 1024 + s0:hf * 1024 + s1],
                        start=True, stop=True), "PE")
            sk = tag("dve", lambda e: e.tensor_tensor(SQB[:, :], ksrc[:, 0:T], ksrc[:, 0:T], op=ALU.mult), "DVE")
            wait("pe", "DVE", sk)
            ktok = None
            for hf in range(2):
                ps = psCa if hf == 0 else psCb
                for (s0, s1) in seg512(0, 1024):
                    ktok = tag("pe", lambda e, ps=ps, hf=hf, s0=s0, s1=s1: e.matmul(
                        ps[0:nheads, s0:s1], ones_lhs, SQB[:, hf * 1024 + s0:hf * 1024 + s1],
                        start=True, stop=True), "PE")
            wait("dve", "PE", ktok)
            d1 = tag("dve", lambda e: e.reduce_max(AXR[0:nheads, 0:1] if False else AUX[0:nheads, 48:49], psA[0:nheads, :], axis=AX), "DVE")
            d2 = tag("dve", lambda e: e.reduce_max(AUX[0:nheads, 49:50], psB[0:nheads, :], axis=AX), "DVE")
            d3 = tag("dve", lambda e: e.reduce_max(AUX[0:nheads, 50:51], psCa[0:nheads, :], axis=AX), "DVE")
            d4 = tag("dve", lambda e: e.reduce_max(AUX[0:nheads, 51:52], psCb[0:nheads, :], axis=AX), "DVE")
            wait("dve", "DVE", d4)
            d5 = tag("dve", lambda e: e.tensor_max(AUX[0:nheads, 48:49], AUX[0:nheads, 48:49], AUX[0:nheads, 49:50]), "DVE")
            d6 = tag("dve", lambda e: e.tensor_max(AUX[0:nheads, 50:51], AUX[0:nheads, 50:51], AUX[0:nheads, 51:52]), "DVE")
            wait("dve", "DVE", d6)
            d7 = tag("dve", lambda e: e.tensor_add(AUX[0:nheads, 48:49], AUX[0:nheads, 48:49], AUX[0:nheads, 50:51]), "DVE")
            wait("dve", "DVE", d7)
            d8 = tag("dve", lambda e: e.tensor_scalar_mul(AUX[0:nheads, 48:49], AUX[0:nheads, 48:49], -scale / 2.0), "DVE")
            wait("pe", "DVE", d8)
            tt = tag("pe", lambda e: e.transpose(psB[0:1, 0:nheads], AUX[0:nheads, 48:49], IDF[0:nheads, 0:nheads]), "PE")
            wait("dve", "PE", tt)
            m0 = tag("dve", lambda e: e.tensor_copy(AXR[0:1, 210:210 + nheads], psB[0:1, 0:nheads]), "DVE")
            toks = []
            for i in range(nheads):
                wait("pe", "DVE", m0)
                m2 = tag("pe", lambda e, i=i: e.matmul(psA[:, i:i + 1], ONER[0:1, :], AXR[0:1, 210 + i:211 + i], start=True, stop=True), "PE")
                wait("dve", "PE", m2)
                m3 = tag("dve", lambda e, i=i, col=aux_cols[i]: e.tensor_copy(AUX[:, col:col + 1], psA[:, i:i + 1]), "DVE")
                toks.append(m3)
            return toks[-1]

        lnt = emit_ln(dep_dve=alpha_tok)
        for p in range(8):
            wait("pe", "DVE", N["DVE"])
            wait("gp", "PE", N["PE"])
            wq = tag("gp", lambda e, p=p: e.dma_start(
                out=WA[:, 0:NC * 128].rearrange("q (c m) -> q c m", c=NC),
                in_=d_in["WQT"].rearrange("(c q) e -> q c e", q=128)[:, :, p * 128:(p + 1) * 128]), "GP", 16)
            wk = tag("gp", lambda e, p=p: e.dma_start(
                out=WA[:, NC * 128:2 * NC * 128].rearrange("q (c m) -> q c m", c=NC),
                in_=d_in["WKT"].rearrange("(c q) e -> q c e", q=128)[:, :, p * 128:(p + 1) * 128]), "GP", 16)
            wv = tag("gp", lambda e, p=p: e.dma_start(
                out=WA[:, 2 * NC * 128:3 * NC * 128].rearrange("q (c m) -> q c m", c=NC),
                in_=d_in["WVT"].rearrange("(c q) e -> q c e", q=128)[:, :, p * 128:(p + 1) * 128]), "GP", 16)
            wait("pe", "DVE", lnt)
            wait("pe", "GP", wv)
            qtok = None
            for c in range(NC):
                for hf in range(2):
                    ps = psA if hf == 0 else psB
                    for (s0, s1) in seg512(0, 1024):
                        qtok = tag("pe", lambda e, c=c, ps=ps, hf=hf, s0=s0, s1=s1: e.matmul(
                            ps[:, s0:s1], WA[:, c * 128:(c + 1) * 128],
                            XLN[:, c * T + hf * 1024 + s0:c * T + hf * 1024 + s1],
                            start=(c == 0), stop=(c == NC - 1)), "PE")
            wait("act", "PE", qtok)
            ktok = None
            for c in range(NC):
                for hf in range(2):
                    ps = psCa if hf == 0 else psCb
                    for (s0, s1) in seg512(0, 1024):
                        ktok = tag("pe", lambda e, c=c, ps=ps, hf=hf, s0=s0, s1=s1: e.matmul(
                            ps[:, s0:s1], WA[:, NC * 128 + c * 128:NC * 128 + (c + 1) * 128],
                            XLN[:, c * T + hf * 1024 + s0:c * T + hf * 1024 + s1],
                            start=(c == 0), stop=(c == NC - 1)), "PE")
            qv = emit_rope(None, (0, 128), QB, (0, 128), 1, None)
            wait("act", "PE", ktok)
            kv = emit_rope(None, (0, 128), KB, (0, 128), 1, None, s1=psCa, s2=psCb)
            wait("pe", "GP", wv)
            wait("pe", "DVE", vbones)
            wait("pe", "DVE", kv)
            vtoks = {}
            for i in range(NT):
                ps = psCa if i < 8 else psCb
                off = (i % 8) * 128
                vt = None
                for c in range(NC):
                    vt = tag("pe", lambda e, c=c, i=i, ps=ps, off=off: e.matmul(
                        ps[:, off:off + 128], XLN[:, c * T + i * 128:c * T + i * 128 + 128],
                        WA[:, 2 * NC * 128 + c * 128:2 * NC * 128 + (c + 1) * 128],
                        start=(c == 0), stop=(c == NC - 1)), "PE")
                vtoks[i] = vt
            va = None
            wait("act", "PE", vtoks[NT - 1])
            for i in range(NT):
                ps = psCa if i < 8 else psCb
                off = (i % 8) * 128
                va = tag("act", lambda e, i=i, ps=ps, off=off: e.copy(
                    VB[:, :].rearrange("p (b k) -> p b k", k=128)[:, 2 * i:2 * i + 2, 0:64],
                    ps[:, off:off + 128].rearrange("p (a k) -> p a k", a=2)), "ACTT")
            wait("dve", "DVE", qv); wait("dve", "DVE", kv); wait("dve", "ACTT", va)
            mb = emit_norm_bound(QB, None, KB, None, SCALE_A, [32, 33], ONE2[:, 0:2])
            wait("pe", "DVE", mb)
            wait("act", "DVE", mb)
            for hf in range(2):
                wait("pe", "DVE", N["DVE"])
                last_pv = {0: None, 1: None}
                exp_tok = {0: None, 1: None}
                for tk in range(NT):
                    stoks = {}
                    for hh in range(2):
                        ps = psA if hh == 0 else psB
                        if exp_tok[hh] is not None:
                            wait("pe", "ACTT", exp_tok[hh])
                        stok = None
                        for (s0, s1) in seg512(0, 1024):
                            stok = tag("pe", lambda e, ps=ps, hh=hh, tk=tk, hf=hf, s0=s0, s1=s1: e.matmul(
                                ps[:, s0:s1], KB[64 * hh:64 * hh + 64, tk * 128:(tk + 1) * 128],
                                QB[64 * hh:64 * hh + 64, hf * 1024 + s0:hf * 1024 + s1],
                                start=True, stop=True), "PE")
                        stoks[hh] = stok
                    for hh in range(2):
                        ps = psA if hh == 0 else psB
                        wait("act", "PE", stoks[hh])
                        exp_tok[hh] = tag("act", lambda e, ps=ps, hh=hh: e.activation(
                            PT[:, hh * 1024:(hh + 1) * 1024], ps[:, :], ACT.Exp,
                            bias=AUX[:, 32 + hh:33 + hh], scale=SCALE_A), "ACTT")
                        psc = psCa if hh == 0 else psCb
                        wait("pe", "ACTT", exp_tok[hh])
                        for (s0, s1) in seg512(0, 1024):
                            last_pv[hh] = tag("pe", lambda e, psc=psc, hh=hh, tk=tk, s0=s0, s1=s1: e.matmul(
                                psc[:, s0:s1],
                                VB[:, :].rearrange("p (b k) -> p b k", k=128)[:, 2 * tk + hh, :],
                                PT[:, hh * 1024 + s0:hh * 1024 + s1],
                                start=(tk == 0), stop=(tk == NT - 1)), "PE")
                for hh in range(2):
                    psc = psCa if hh == 0 else psCb
                    wait("dve", "PE", last_pv[hh])
                    wait("dve", "DVE", N["DVE"])
                    e1 = tag("dve", lambda e, psc=psc: e.reciprocal(FS[0:64, 0:1024], psc[64:128, :]), "DVE")
                    wait("dve", "DVE", e1)
                    e2 = tag("dve", lambda e, psc=psc, hh=hh, p=p, hf=hf: e.tensor_tensor(
                        CTXT[64 * hh:64 * hh + 64, p * T + hf * 1024:p * T + hf * 1024 + 1024],
                        psc[0:64, :], FS[0:64, 0:1024], op=ALU.mult), "DVE")
                    if hf == 0:
                        wait("pe", "DVE", e2)
            ctx_done = N["DVE"]
        wait("pe", "DVE", ctx_done)
        res_tok = None

        for eo in range(NC):
            wait("gp", "PE", N["PE"])
            wo = tag("gp", lambda e, eo=eo: e.dma_start(
                out=WA[:, 3 * NC * 128: 3 * NC * 128 + NC * 128].rearrange("q (c m) -> q c m", c=NC),
                in_=d_in["WOT"].rearrange("(c q) e -> q c e", q=128)[:, :, eo * 128:(eo + 1) * 128]), "GP", 16)
            wait("pe", "GP", wo)
            for hf in range(2):
                ps = psA if hf == 0 else psB
                if res_tok is not None:
                    wait("pe", "DVE", res_tok)
                otok = None
                for c in range(NC):
                    for (s0, s1) in seg512(0, 1024):
                        otok = tag("pe", lambda e, c=c, eo=eo, hf=hf, ps=ps, s0=s0, s1=s1: e.matmul(
                            ps[:, s0:s1], WA[:, 3 * NC * 128 + c * 128:3 * NC * 128 + (c + 1) * 128],
                            CTXT[:, c * T + hf * 1024 + s0:c * T + hf * 1024 + s1],
                            start=(c == 0), stop=(c == NC - 1)), "PE")
                wait("dve", "PE", otok)
                res_tok = tag("dve", lambda e, eo=eo, hf=hf, ps=ps: e.tensor_tensor(
                    XT[:, eo * T + hf * 1024:eo * T + hf * 1024 + 1024],
                    XT[:, eo * T + hf * 1024:eo * T + hf * 1024 + 1024], ps[:, :], op=ALU.add), "DVE")

        def emit_ffn(tag_id):
            wait("act", "DVE", N["DVE"]); wait("act", "PE", N["PE"])
            wait("dve", "PE", N["PE"]); wait("dve", "ACTT", N["ACTT"]); wait("dve", "GP", N["GP"])
            wait("pe", "DVE", N["DVE"]); wait("pe", "ACTT", N["ACTT"])
            wait("gp", "PE", N["PE"]); wait("gp", "DVE", N["DVE"])
            pe_tok = None
            for c in range(NC):
                if pe_tok is not None:
                    wait("act", "PE", pe_tok)
                    wait("dve", "PE", pe_tok)
                a = tag("act", lambda e, c=c: e.copy(SQB[:, :], XT[:, c * T:(c + 1) * T]), "ACTT")
                wait("dve", "ACTT", a)
                sq = tag("dve", lambda e: e.tensor_tensor(RSC[:, :], SQB[:, :], SQB[:, :], op=ALU.mult), "DVE")
                wait("pe", "ACTT", a); wait("pe", "DVE", sq)
                for hfq in range(2):
                    ps = psA if hfq == 0 else psB
                    ps2 = psCa if hfq == 0 else psCb
                    for (s0, s1) in seg512(0, 1024):
                        pe_tok = tag("pe", lambda e, ps=ps, hfq=hfq, s0=s0, s1=s1, c=c: e.matmul(
                            ps[0:1, s0:s1], ONEB[:, 0:1], SQB[:, hfq * 1024 + s0:hfq * 1024 + s1],
                            start=(c == 0), stop=(c == NC - 1)), "PE")
                        pe_tok = tag("pe", lambda e, ps2=ps2, hfq=hfq, s0=s0, s1=s1, c=c: e.matmul(
                            ps2[0:1, s0:s1], ONEB[:, 0:1], RSC[:, hfq * 1024 + s0:hfq * 1024 + s1],
                            start=(c == 0), stop=(c == NC - 1)), "PE")
            wait("dve", "PE", pe_tok)
            r2 = None
            for hfq in range(2):
                ps = psA if hfq == 0 else psB
                ps2 = psCa if hfq == 0 else psCb
                h0 = hfq * 1024
                h1 = (1 - hfq) * 1024
                r = tag("dve", lambda e, ps=ps, h0=h0: e.tensor_scalar_mul(ROWS[0:1, 2048 + h0:2048 + h0 + 1024], ps[0:1, :], 1.0 / D), "DVE")
                wait("dve", "DVE", r)
                r = tag("dve", lambda e, h0=h0: e.tensor_tensor(FS[0:1, h0:h0 + 1024], ROWS[0:1, 2048 + h0:2048 + h0 + 1024], ROWS[0:1, 2048 + h0:2048 + h0 + 1024], op=ALU.mult), "DVE")
                wait("dve", "DVE", r)
                r = tag("dve", lambda e, ps2=ps2, h0=h0: e.scalar_tensor_tensor(FS[0:1, h0:h0 + 1024], ps2[0:1, :], 1.0 / D, FS[0:1, h0:h0 + 1024], op0=ALU.mult, op1=ALU.subtract), "DVE")
                wait("act", "DVE", r)
                r = tag("act", lambda e, h0=h0, h1=h1: e.activation(FS[0:1, h1:h1 + 1024], FS[0:1, h0:h0 + 1024], ACT.Ln, bias=EPS_LN, scale=1.0), "ACTT")
                wait("act", "ACTT", r)
                r = tag("act", lambda e, h0=h0, h1=h1: e.activation(ROWS[0:1, h0:h0 + 1024], FS[0:1, h1:h1 + 1024], ACT.Exp, scale=-0.5), "ACTT")
                wait("act", "ACTT", r)
                r = tag("act", lambda e, h1=h1: e.activation(FS[0:1, h1:h1 + 1024], FS[0:1, h1:h1 + 1024], ACT.Exp, scale=-1.0), "ACTT")
                wait("dve", "ACTT", r)
                r = tag("dve", lambda e, h0=h0, h1=h1: e.tensor_tensor(FS[0:1, h0:h0 + 1024], FS[0:1, h0:h0 + 1024], FS[0:1, h1:h1 + 1024], op=ALU.mult), "DVE")
                wait("act", "DVE", r)
                r = tag("act", lambda e, h0=h0: e.activation(FS[0:1, h0:h0 + 1024], FS[0:1, h0:h0 + 1024], ACT.Ln, bias=EPS_LN, scale=1.0), "ACTT")
                wait("act", "ACTT", r)
                r = tag("act", lambda e, h0=h0, h1=h1: e.activation(FS[0:1, h1:h1 + 1024], FS[0:1, h0:h0 + 1024], ACT.Exp, scale=-0.5), "ACTT")
                wait("dve", "ACTT", r)
                r = tag("dve", lambda e, h0=h0, h1=h1: e.tensor_tensor(ROWS[0:1, h0:h0 + 1024], ROWS[0:1, h0:h0 + 1024], FS[0:1, h1:h1 + 1024], op=ALU.mult), "DVE")
                wait("dve", "DVE", r)
                r2 = tag("dve", lambda e, h0=h0: e.tensor_tensor(ROWS[0:1, 2048 + h0:2048 + h0 + 1024], ROWS[0:1, 2048 + h0:2048 + h0 + 1024], ROWS[0:1, h0:h0 + 1024], op=ALU.mult), "DVE")
            wait("pe", "DVE", r2)
            ptok = None
            for hfq in range(2):
                ps1 = psA if hfq == 0 else psB
                ps2 = psCa if hfq == 0 else psCb
                for (s0, s1) in seg512(0, 1024):
                    ptok = tag("pe", lambda e, ps1=ps1, hfq=hfq, s0=s0, s1=s1: e.matmul(
                        ps1[:, s0:s1], ONER[0:1, :], ROWS[0:1, hfq * 1024 + s0:hfq * 1024 + s1], start=True, stop=True), "PE")
                    ptok = tag("pe", lambda e, ps2=ps2, hfq=hfq, s0=s0, s1=s1: e.matmul(
                        ps2[:, s0:s1], ONER[0:1, :], ROWS[0:1, 2048 + hfq * 1024 + s0:2048 + hfq * 1024 + s1], start=True, stop=True), "PE")
            wait("dve", "PE", ptok)
            am = tag("dve", lambda e: e.memset(AUX[:, 52:53], 0.0), "DVE")
            for c in range(NC):
                for hfq in range(2):
                    ps1 = psA if hfq == 0 else psB
                    ps2 = psCa if hfq == 0 else psCb
                    wait("dve", "ACTT", N["ACTT"])
                    u1 = tag("dve", lambda e, c=c, hfq=hfq, ps1=ps1: e.tensor_tensor(
                        FS[:, hfq * 1024:hfq * 1024 + 1024],
                        XT[:, c * T + hfq * 1024:c * T + hfq * 1024 + 1024], ps1[:, :], op=ALU.mult), "DVE")
                    wait("dve", "DVE", u1)
                    u2 = tag("dve", lambda e, c=c, hfq=hfq, ps2=ps2: e.tensor_tensor(
                        FS[:, hfq * 1024:hfq * 1024 + 1024],
                        FS[:, hfq * 1024:hfq * 1024 + 1024], ps2[:, :], op=ALU.subtract), "DVE")
                    wait("dve", "DVE", u2)
                    u3 = tag("dve", lambda e, c=c, hfq=hfq: e.tensor_scalar(
                        FS[:, (1 - hfq) * 1024:(1 - hfq) * 1024 + 1024], FS[:, hfq * 1024:hfq * 1024 + 1024],
                        AUX[:, c:c + 1], AUX[:, 8 + c:9 + c], op0=ALU.mult, op1=ALU.add), "DVE")
                    wait("dve", "DVE", u3)
                    u4 = tag("dve", lambda e, hfq=hfq: e.reduce_max(
                        AUX[:, 53:54], FS[:, (1 - hfq) * 1024:(1 - hfq) * 1024 + 1024], axis=AX,
                        apply_absolute_value=True), "DVE")
                    wait("dve", "DVE", u4)
                    am = tag("dve", lambda e: e.tensor_max(AUX[:, 52:53], AUX[:, 52:53], AUX[:, 53:54]), "DVE")
                    wait("dve", "DVE", am)
            wait("pe", "DVE", am)
            pm = tag("pe", lambda e: e.transpose(psA[0:1, 0:128], AUX[:, 52:53], IDF[:, :]), "PE")
            wait("dve", "PE", pm)
            gm = tag("dve", lambda e: e.reduce_max(AXR[0:1, 220:221], psA[0:1, 0:128], axis=AX), "DVE")
            wait("dve", "DVE", gm)
            gm2 = tag("dve", lambda e: e.tensor_single_scalar(AXR[0:1, 220:221], AXR[0:1, 220:221], 1e-8, op=ALU.max), "DVE")
            wait("gp", "DVE", gm2)
            g1 = tag("gp", lambda e: e.dma_start(out=cc_in.ap(), in_=AXR[0:1, 220:228]), "GP", 16)
            wait("gp", "GP", g1)
            g2 = tag("gp", lambda e: e.collective_compute(
                "AllReduce", ALU.max, replica_groups=[list(range(NCORES))],
                ins=[cc_in.ap().opt()], outs=[cc_out.ap().opt()]), "CC")
            wait("gp", "CC", g2)
            g3 = tag("gp", lambda e: e.dma_start(out=CCS[:, :], in_=cc_out.ap()), "GP", 16)
            wait("dve", "GP", g3)
            i1 = tag("dve", lambda e: e.reciprocal(AXR[0:1, 221:222], CCS[0:1, 0:1]), "DVE")
            wait("dve", "DVE", i1)
            i2 = tag("dve", lambda e: e.tensor_scalar_mul(AXR[0:1, 221:222], AXR[0:1, 221:222], 127.0), "DVE")
            i3 = tag("dve", lambda e: e.tensor_scalar_mul(AXR[0:1, 222:223], CCS[0:1, 0:1], 1.0 / 127.0), "DVE")
            wait("dve", "DVE", i3)
            i4 = tag("dve", lambda e: e.tensor_tensor(AXR[0:1, 223:224], AXR[0:1, 222:223], AXR[0:1, 205:206], op=ALU.mult), "DVE")
            wait("pe", "DVE", i4)
            i5 = tag("pe", lambda e: e.matmul(psA[:, 0:1], ONER[0:1, :], AXR[0:1, 221:222], start=True, stop=True), "PE")
            i6 = tag("pe", lambda e: e.matmul(psA[:, 1:2], ONER[0:1, :], AXR[0:1, 223:224], start=True, stop=True), "PE")
            wait("dve", "PE", i6)
            i7 = tag("dve", lambda e: e.tensor_copy(AUX[:, 54:55], psA[:, 0:1]), "DVE")
            i8 = tag("dve", lambda e: e.tensor_copy(AUX[:, 55:56], psA[:, 1:2]), "DVE")
            wait("dve", "DVE", i8)
            gi = tag("dve", lambda e: e.tensor_scalar(AUX[:, 16:16 + NC], AUX[:, 0:NC], AUX[:, 54:55], None, op0=ALU.mult), "DVE")
            bi = tag("dve", lambda e: e.tensor_scalar(AUX[:, 24:24 + NC], AUX[:, 8:8 + NC], AUX[:, 54:55], None, op0=ALU.mult), "DVE")
            wait("dve", "DVE", bi)
            wait("pe", "DVE", bi)
            ptok = None
            for hfq in range(2):
                ps1 = psA if hfq == 0 else psB
                ps2 = psCa if hfq == 0 else psCb
                for (s0, s1) in seg512(0, 1024):
                    ptok = tag("pe", lambda e, ps1=ps1, hfq=hfq, s0=s0, s1=s1: e.matmul(
                        ps1[:, s0:s1], ONER[0:1, :], ROWS[0:1, hfq * 1024 + s0:hfq * 1024 + s1], start=True, stop=True), "PE")
                    ptok = tag("pe", lambda e, ps2=ps2, hfq=hfq, s0=s0, s1=s1: e.matmul(
                        ps2[:, s0:s1], ONER[0:1, :], ROWS[0:1, 2048 + hfq * 1024 + s0:2048 + hfq * 1024 + s1], start=True, stop=True), "PE")
            wait("dve", "PE", ptok)
            xq_tok = None
            for c in range(NC):
                for hfq in range(2):
                    ps1 = psA if hfq == 0 else psB
                    ps2 = psCa if hfq == 0 else psCb
                    wait("dve", "ACTT", N["ACTT"])
                    u1 = tag("dve", lambda e, c=c, hfq=hfq, ps1=ps1: e.tensor_tensor(
                        FS[:, hfq * 1024:hfq * 1024 + 1024],
                        XT[:, c * T + hfq * 1024:c * T + hfq * 1024 + 1024], ps1[:, :], op=ALU.mult), "DVE")
                    wait("dve", "DVE", u1)
                    u2 = tag("dve", lambda e, c=c, hfq=hfq, ps2=ps2: e.tensor_tensor(
                        FS[:, hfq * 1024:hfq * 1024 + 1024],
                        FS[:, hfq * 1024:hfq * 1024 + 1024], ps2[:, :], op=ALU.subtract), "DVE")
                    wait("dve", "DVE", u2)
                    u3 = tag("dve", lambda e, c=c, hfq=hfq: e.tensor_scalar(
                        FS[:, (1 - hfq) * 1024:(1 - hfq) * 1024 + 1024], FS[:, hfq * 1024:hfq * 1024 + 1024],
                        AUX[:, 16 + c:17 + c], AUX[:, 24 + c:25 + c], op0=ALU.mult, op1=ALU.add), "DVE")
                    wait("dve", "DVE", u3)
                    for qq in range(2):
                        if xq_tok is not None:
                            wait("dve", "DVE", xq_tok)
                        u4 = tag("dve", lambda e, hfq=hfq, qq=qq: e.tensor_copy(
                            ICST[:, :], FS[:, (1 - hfq) * 1024 + qq * 512:(1 - hfq) * 1024 + qq * 512 + 512]), "DVE")
                        wait("dve", "DVE", u4)
                        xq_tok = tag("dve", lambda e, c=c, hfq=hfq, qq=qq: e.tensor_copy(
                            XLN[:, c * T + hfq * 1024 + qq * 512:c * T + hfq * 1024 + qq * 512 + 512], ICST[:, :]), "DVE")
            res = None
            for ot in range(NC):
                wslot = 3 * NC * 128
                wait("gp", "PE", N["PE"])
                wt = tag("gp", lambda e, ot=ot, wslot=wslot: e.dma_start(
                    out=WA[:, wslot:wslot + NC * 128].rearrange("q (c m) -> q c m", c=NC),
                    in_=wscr.ap().rearrange("(c q) o -> q c o", q=128)[:, :, ot * 128:(ot + 1) * 128]), "GP", 16)
                wait("pe", "GP", wt)
                wait("pe", "DVE", xq_tok)
                for hfq in range(2):
                    ps = psA if hfq == 0 else psB
                    if res is not None:
                        wait("pe", "DVE", res)
                    mt = None
                    for c in range(NC):
                        for (s0, s1) in seg512(0, 1024):
                            mt = tag("pe", lambda e, c=c, ot=ot, hfq=hfq, ps=ps, s0=s0, s1=s1, wslot=wslot: e.matmul(
                                ps[:, s0:s1], WA[:, wslot + c * 128:wslot + (c + 1) * 128],
                                XLN[:, c * T + hfq * 1024 + s0:c * T + hfq * 1024 + s1],
                                start=(c == 0), stop=(c == NC - 1)), "PE")
                    wait("dve", "PE", mt)
                    res = tag("dve", lambda e, ot=ot, hfq=hfq, ps=ps: e.scalar_tensor_tensor(
                        XT[:, ot * T + hfq * 1024:ot * T + hfq * 1024 + 1024], ps[:, :],
                        AUX[:, 55:56], XT[:, ot * T + hfq * 1024:ot * T + hfq * 1024 + 1024],
                        op0=ALU.mult, op1=ALU.add), "DVE")
            return res

        ffn1 = emit_ffn(1)

        wait("gp", "DVE", N["DVE"]); wait("gp", "PE", N["PE"]); wait("gp", "ACTT", N["ACTT"])
        tb = tag("gp", lambda e: e.dma_start(out=TABC[:, :], in_=d_in["CLb"]), "GP", 16)
        tb = tag("gp", lambda e: e.dma_start(out=TABS[:, :], in_=d_in["SLb"]), "GP", 16)
        lnt = emit_ln(dep_dve=ffn1)
        kzero = tag("dve", lambda e: e.memset(KB[32:64, 0:T], 0.0), "DVE")
        kzero = tag("dve", lambda e: e.memset(KB[96:128, 0:T], 0.0), "DVE")
        kzero = tag("dve", lambda e: e.memset(KB[0:32, T:2 * T], 0.0), "DVE")
        kzero = tag("dve", lambda e: e.memset(KB[64:96, T:2 * T], 0.0), "DVE")
        for p in range(8):
            g = p
            wait("pe", "DVE", N["DVE"])
            wait("gp", "PE", N["PE"])
            wq = tag("gp", lambda e, p=p: e.dma_start(
                out=WA[:, 0:NC * 128].rearrange("q (c m) -> q c m", c=NC),
                in_=d_in["Wq"].rearrange("(c q) e -> q c e", q=128)[:, :, p * 128:(p + 1) * 128]), "GP", 16)
            wk = tag("gp", lambda e, g=g: e.dma_start(
                out=WA[:, NC * 128:NC * 128 + NC * 64].rearrange("q (c m) -> q c m", c=NC),
                in_=d_in["Wk"].rearrange("(c q) e -> q c e", q=128)[:, :, g * 64:(g + 1) * 64]), "GP", 16)
            wk = tag("gp", lambda e, g=g: e.dma_start(
                out=WA[:, NC * 128 + NC * 64:NC * 128 + NC * 128].rearrange("q (c m) -> q c m", c=NC),
                in_=d_in["Wk"].rearrange("(c q) e -> q c e", q=128)[:, :, g * 64:(g + 1) * 64]), "GP", 16)
            wv = tag("gp", lambda e, g=g: e.dma_start(
                out=WA[:, 2 * NC * 128:2 * NC * 128 + NC * 64].rearrange("q (c m) -> q c m", c=NC),
                in_=d_in["Wv"].rearrange("(c q) e -> q c e", q=128)[:, :, g * 64:(g + 1) * 64]), "GP", 16)
            wait("pe", "DVE", lnt)
            wait("pe", "GP", wv)
            wait("pe", "GP", wk)
            wait("pe", "GP", tb)
            qtok = None
            for c in range(NC):
                for hfq in range(2):
                    ps = psA if hfq == 0 else psB
                    for (s0, s1) in seg512(0, 1024):
                        qtok = tag("pe", lambda e, c=c, ps=ps, hfq=hfq, s0=s0, s1=s1: e.matmul(
                            ps[:, s0:s1], WA[:, c * 128:(c + 1) * 128],
                            XLN[:, c * T + hfq * 1024 + s0:c * T + hfq * 1024 + s1],
                            start=(c == 0), stop=(c == NC - 1)), "PE")
            wait("act", "PE", qtok)
            ktok = None
            for c in range(NC):
                for hfq in range(2):
                    ps = psCa if hfq == 0 else psCb
                    for (s0, s1) in seg512(0, 1024):
                        ktok = tag("pe", lambda e, c=c, ps=ps, hfq=hfq, s0=s0, s1=s1: e.matmul(
                            ps[0:64, s0:s1], WA[:, NC * 128 + c * 64:NC * 128 + (c + 1) * 64],
                            XLN[:, c * T + hfq * 1024 + s0:c * T + hfq * 1024 + s1],
                            start=(c == 0), stop=(c == NC - 1)), "PE")
            qv = emit_rope(None, (0, 128), QB, (0, 128), 16, None)
            wait("act", "PE", ktok)
            kv = emit_rope(None, (0, 64), RSC, (0, 64), 16, None, s1=psCa, s2=psCb)
            wait("act", "DVE", kv)
            wait("act", "DVE", kzero)
            wait("act", "PE", N["PE"])
            wait("gp", "DVE", kv)
            wait("gp", "PE", N["PE"])
            ka = tag("act", lambda e: e.copy(KB[0:32, 0:T], RSC[0:32, :]), "ACTT")
            ka = tag("act", lambda e: e.copy(KB[32:64, T:2 * T], RSC[32:64, :]), "ACTT")
            kg = tag("gp", lambda e: e.dma_start(out=KB[64:96, 0:T], in_=RSC[0:32, :]), "GP", 16)
            kg = tag("gp", lambda e: e.dma_start(out=KB[96:128, T:2 * T], in_=RSC[32:64, :]), "GP", 16)
            kb2 = ka
            wait("pe", "GP", kg)
            wait("dve", "GP", kg)
            wait("pe", "GP", wv)
            wait("pe", "DVE", vbones)
            wait("pe", "DVE", kv)
            vtoks = {}
            for i in range(NT):
                ps = psCa if i < 8 else psCb
                off = (i % 8) * 128
                vt = None
                for c in range(NC):
                    vt = tag("pe", lambda e, c=c, i=i, ps=ps, off=off: e.matmul(
                        ps[:, off:off + 64], XLN[:, c * T + i * 128:c * T + i * 128 + 128],
                        WA[:, 2 * NC * 128 + c * 64:2 * NC * 128 + (c + 1) * 64],
                        start=(c == 0), stop=(c == NC - 1)), "PE")
                vtoks[i] = vt
            va = None
            wait("act", "PE", vtoks[NT - 1])
            for i in range(NT):
                ps = psCa if i < 8 else psCb
                off = (i % 8) * 128
                va = tag("act", lambda e, i=i, ps=ps, off=off: e.copy(
                    VB[:, :].rearrange("p (b k) -> p b k", k=128)[:, i, 0:64], ps[:, off:off + 64]), "ACTT")
            wait("dve", "DVE", qv); wait("dve", "ACTT", kb2); wait("dve", "ACTT", va)
            mb = emit_norm_bound(QB, None, KB, None, SCALE_B, [34, 35, 36, 37], ONE4[:, 0:4])
            wait("pe", "DVE", mb)
            wait("act", "DVE", mb)
            for pair in range(2):
                qoff = 64 * pair
                for hfq in range(2):
                    wait("pe", "DVE", N["DVE"])
                    t0 = hfq * 1024
                    t1 = t0 + 1024
                    ntk = (t1 + 127) // 128
                    last_pv = {0: None, 1: None}
                    exp_tok = {0: None, 1: None}
                    for tk in range(ntk):
                        trim = max(t0, tk * 128)
                        if trim >= t1:
                            continue
                        stoks = {}
                        for hh in range(2):
                            ps = psA if hh == 0 else psB
                            if exp_tok[hh] is not None:
                                wait("pe", "ACTT", exp_tok[hh])
                            stok = None
                            for (s0, s1) in seg512(trim - t0, 1024):
                                stok = tag("pe", lambda e, ps=ps, hh=hh, tk=tk, t0=t0, s0=s0, s1=s1, qoff=qoff: e.matmul(
                                    ps[:, s0:s1], KB[qoff:qoff + 64, hh * T + tk * 128:hh * T + (tk + 1) * 128],
                                    QB[qoff:qoff + 64, t0 + s0:t0 + s1],
                                    start=True, stop=True), "PE")
                            if 128 * tk >= t0:
                                stok = tag("pe", lambda e, ps=ps, tk=tk, t0=t0: e.matmul(
                                    ps[:, tk * 128 - t0:tk * 128 - t0 + 128], MSKB[:, :], IDB[:, :],
                                    start=False, stop=True, skip_group_check=True), "PE")
                            stoks[hh] = stok
                        for hh in range(2):
                            ps = psA if hh == 0 else psB
                            wait("act", "PE", stoks[hh])
                            mcol = 34 + 2 * pair + hh
                            exp_tok[hh] = tag("act", lambda e, ps=ps, hh=hh, trim=trim, t0=t0, mcol=mcol: e.activation(
                                PT[:, hh * 1024 + trim - t0:(hh + 1) * 1024], ps[:, trim - t0:1024], ACT.Exp,
                                bias=AUX[:, mcol:mcol + 1], scale=SCALE_B), "ACTT")
                            mz = None
                            if trim > t0:
                                if last_pv[hh] is not None:
                                    wait("dve", "PE", last_pv[hh])
                                mz = tag("dve", lambda e, hh=hh, trim=trim, t0=t0: e.memset(
                                    PT[:, hh * 1024:hh * 1024 + trim - t0], 0.0), "DVE")
                            psc = psCa if hh == 0 else psCb
                            wait("pe", "ACTT", exp_tok[hh])
                            if mz is not None:
                                wait("pe", "DVE", mz)
                            for (s0, s1) in seg512(0, 1024):
                                last_pv[hh] = tag("pe", lambda e, psc=psc, hh=hh, tk=tk, s0=s0, s1=s1, ntk=ntk: e.matmul(
                                    psc[:, s0:s1],
                                    VB[:, :].rearrange("p (b k) -> p b k", k=128)[:, tk, :],
                                    PT[:, hh * 1024 + s0:hh * 1024 + s1],
                                    start=(tk == 0), stop=(tk == ntk - 1)), "PE")
                    wait("dve", "PE", last_pv[0]); wait("dve", "PE", last_pv[1])
                    e1 = tag("dve", lambda e: e.reciprocal(FS[0:64, 0:1024], psCa[64:128, :]), "DVE")
                    wait("dve", "DVE", e1)
                    e2 = tag("dve", lambda e: e.tensor_tensor(FS[0:64, 1024:2048], psCa[0:64, :], FS[0:64, 0:1024], op=ALU.mult), "DVE")
                    e3 = tag("dve", lambda e: e.reciprocal(FS[64:128, 0:1024], psCb[64:128, :]), "DVE")
                    wait("dve", "DVE", e3)
                    e4 = tag("dve", lambda e: e.tensor_tensor(FS[0:64, 0:1024], psCb[0:64, :], FS[64:128, 0:1024], op=ALU.mult), "DVE")
                    wait("dve", "DVE", e4)
                    e5 = tag("dve", lambda e: e.scalar_tensor_tensor(
                        FS[0:64, 0:1024], FS[0:64, 0:1024], AUX[0:64, 38:39], FS[0:64, 1024:2048],
                        op0=ALU.mult, op1=ALU.add), "DVE")
                    wait("dve", "DVE", e5)
                    e6 = tag("dve", lambda e: e.tensor_tensor(SQB[0:64, 0:1024], FS[0:64, 0:1024], FS[0:64, 0:1024], op=ALU.mult), "DVE")
                    wait("pe", "DVE", e6)
                    rtok = None
                    for (s0, s1) in seg512(0, 1024):
                        rtok = tag("pe", lambda e, s0=s0, s1=s1: e.matmul(
                            psA[0:1, s0:s1], ONEB[0:64, 0:1], SQB[0:64, s0:s1], start=True, stop=True), "PE")
                    wait("act", "PE", rtok)
                    r1 = tag("act", lambda e: e.activation(ROWS[0:1, 3072:4096], psA[0:1, :], ACT.Ln, bias=EPS_RMS, scale=1.0 / 64.0), "ACTT")
                    wait("act", "ACTT", r1)
                    r2 = tag("act", lambda e: e.activation(ROWS[0:1, 2048:3072], ROWS[0:1, 3072:4096], ACT.Exp, scale=-0.5), "ACTT")
                    wait("pe", "ACTT", r2)
                    rep = None
                    for (s0, s1) in seg512(0, 1024):
                        rep = tag("pe", lambda e, s0=s0, s1=s1: e.matmul(
                            psB[0:64, s0:s1], AXR[0:1, 256:320], ROWS[0:1, 2048 + s0:2048 + s1], start=True, stop=True), "PE")
                    wait("dve", "PE", rep)
                    h = 2 * p + pair
                    crow = 64 * (h % 2)
                    cch = h // 2
                    e7 = tag("dve", lambda e, crow=crow, cch=cch, hfq=hfq: e.tensor_tensor(
                        CTXT[crow:crow + 64, cch * T + hfq * 1024:cch * T + hfq * 1024 + 1024],
                        FS[0:64, 0:1024], psB[0:64, :], op=ALU.mult), "DVE")
                    wait("pe", "DVE", e7)
            ctx_done_b = N["DVE"]
        wait("pe", "DVE", ctx_done_b)
        for eo in range(NC):
            wait("gp", "PE", N["PE"])
            wo = tag("gp", lambda e, eo=eo: e.dma_start(
                out=WA[:, 3 * NC * 128: 3 * NC * 128 + NC * 128].rearrange("q (c m) -> q c m", c=NC),
                in_=d_in["Wo"].rearrange("(c q) e -> q c e", q=128)[:, :, eo * 128:(eo + 1) * 128]), "GP", 16)
            wait("pe", "GP", wo)
            for hfq in range(2):
                ps = psA if hfq == 0 else psB
                if res_tok is not None:
                    wait("pe", "DVE", res_tok)
                otok = None
                for c in range(NC):
                    for (s0, s1) in seg512(0, 1024):
                        otok = tag("pe", lambda e, c=c, eo=eo, hfq=hfq, ps=ps, s0=s0, s1=s1: e.matmul(
                            ps[:, s0:s1], WA[:, 3 * NC * 128 + c * 128:3 * NC * 128 + (c + 1) * 128],
                            CTXT[:, c * T + hfq * 1024 + s0:c * T + hfq * 1024 + s1],
                            start=(c == 0), stop=(c == NC - 1)), "PE")
                wait("dve", "PE", otok)
                res_tok = tag("dve", lambda e, eo=eo, hfq=hfq, ps=ps: e.tensor_tensor(
                    XT[:, eo * T + hfq * 1024:eo * T + hfq * 1024 + 1024],
                    XT[:, eo * T + hfq * 1024:eo * T + hfq * 1024 + 1024], ps[:, :], op=ALU.add), "DVE")

        ffn2 = emit_ffn(2)

        wait("sp", "DVE", ffn2)
        sttok = None
        for c in range(NC):
            sttok = tag("sp", lambda e, c=c: e.dma_start(
                out=out_d[c * 128:(c + 1) * 128, :], in_=XT[:, c * T:(c + 1) * T]), "ST", 16)
        wait("sp", "ST", sttok)

        block = es.enter_context(nc.Block())

        @block.sync
        def _(sync):
            for fn in Q["sp"]:
                fn(sync)

        @block.tensor
        def _(tensor):
            for fn in Q["pe"]:
                fn(tensor)

        @block.scalar
        def _(scalar):
            for fn in Q["act"]:
                fn(scalar)

        @block.vector
        def _(vector):
            for fn in Q["dve"]:
                fn(vector)

        @block.gpsimd
        def _(gpsimd):
            for fn in Q["gp"]:
                fn(gpsimd)

    return nc


_NC_CACHE = None


def kernel(**inputs):
    global _NC_CACHE
    x = np.asarray(inputs["L0_attn_a_x"])
    CLa, SLa, CLb, SLb, ID, MSK = _host_consts()
    common = {
        "WQT": np.ascontiguousarray(np.asarray(inputs["L0_attn_a_W_Q"]).T),
        "WKT": np.ascontiguousarray(np.asarray(inputs["L0_attn_a_W_K"]).T),
        "WVT": np.ascontiguousarray(np.asarray(inputs["L0_attn_a_W_V"]).T),
        "WOT": np.ascontiguousarray(np.asarray(inputs["L0_attn_a_W_O"]).T),
        "Wq": np.ascontiguousarray(np.asarray(inputs["L0_attn_b_Wq"])),
        "Wk": np.ascontiguousarray(np.asarray(inputs["L0_attn_b_Wk"])),
        "Wv": np.ascontiguousarray(np.asarray(inputs["L0_attn_b_Wv"])),
        "Wo": np.ascontiguousarray(np.asarray(inputs["L0_attn_b_Wo"])),
        "FWT": np.ascontiguousarray(np.asarray(inputs["L0_ffn_weight"]).T),
        "CLa": CLa, "SLa": SLa, "CLb": CLb, "SLb": SLb, "IDC": ID, "MSK": MSK,
        "lq1": np.asarray(inputs["L0_attn_b_lambda_q1"]).reshape(1, 32).astype(np.float32),
        "lk1": np.asarray(inputs["L0_attn_b_lambda_k1"]).reshape(1, 32).astype(np.float32),
        "lq2": np.asarray(inputs["L0_attn_b_lambda_q2"]).reshape(1, 32).astype(np.float32),
        "lk2": np.asarray(inputs["L0_attn_b_lambda_k2"]).reshape(1, 32).astype(np.float32),
        "rmsw": np.asarray(inputs["L0_attn_b_rms_w"]).reshape(1, 64).astype(np.float32),
        "gam": np.asarray(inputs["L0_ffn_gamma"]).reshape(1, D).astype(np.float32),
        "bet": np.asarray(inputs["L0_ffn_beta"]).reshape(1, D).astype(np.float32),
    }
    in_maps = []
    for c in range(NCORES):
        m = dict(common)
        m["xT"] = np.ascontiguousarray(x[c].T)
        in_maps.append(m)
    if _NC_CACHE is None:
        _NC_CACHE = build()
    res = run_bass_kernel_spmd(_NC_CACHE, in_maps, core_ids=list(range(NCORES)))
    out = np.stack([np.ascontiguousarray(r["outT"].T) for r in res.results], axis=0)
    return out.astype(np.float32)


if __name__ == "__main__":
    import json
    np.random.seed(0)
    pass


# version 06
# speedup vs baseline: 1.2102x; 1.2102x over previous
import sys
sys.path.insert(0, "/opt/trn_rl_repo")
import math
import numpy as np
import ml_dtypes
import concourse.bass as bass
import concourse.mybir as mybir
from concourse.bass_utils import run_bass_kernel_spmd

F32 = mybir.dt.float32
BF16 = mybir.dt.bfloat16
I32 = mybir.dt.int32
AX = mybir.AxisListType.X
ALU = mybir.AluOpType
ACT = mybir.ActivationFunctionType

NCORES = 8
T = 2048
D = 1024
NT = T // 128
NC = D // 128
LAMBDA_INIT = 0.8 - 0.6 * math.exp(-0.3 * 12)
SCALE_A = 0.125
SCALE_B = 1.0 / math.sqrt(32.0)
EPS_LN = 1e-5
EPS_RMS = 1e-6


def _host_consts():
    i = np.arange(128)
    inv_a = 10000.0 ** (-2.0 * np.arange(32, dtype=np.float64) / 64.0)
    t = np.arange(T, dtype=np.float64)
    hd_a = (i % 64) // 2
    ang_a = np.outer(inv_a[hd_a], t)
    CLa = np.cos(ang_a)
    sgn_a = np.where(i % 2 == 0, -1.0, 1.0)
    SLa = np.sin(ang_a) * sgn_a[:, None]
    inv_b = 10000.0 ** (-2.0 * np.arange(16, dtype=np.float64) / 32.0)
    ang_b = np.outer(inv_b[i % 16], t)
    CLb = np.cos(ang_b)
    sgn_b = np.where((i % 32) < 16, 1.0, -1.0)
    SLb = np.sin(ang_b) * sgn_b[:, None]
    ID = np.eye(128, dtype=np.float32)
    maskT = np.where(np.arange(128)[None, :] > np.arange(128)[:, None],
                     -1000.0, 0.0).astype(np.float32)
    return (CLa.astype(np.float32), SLa.astype(np.float32),
            CLb.astype(np.float32), SLb.astype(np.float32), ID, maskT)


def build():
    nc = bass.Bass("TRN2", target_bir_lowering=False, debug=False)
    d_in = {}
    d_in["xT"] = nc.dram_tensor("xT", [D, T], F32, kind="ExternalInput").ap()
    for n, shp in [("WQT", [D, D]), ("WKT", [D, D]), ("WVT", [D, D]), ("WOT", [D, D]),
                   ("Wq", [D, D]), ("Wk", [D, 512]), ("Wv", [D, 512]), ("Wo", [D, D]),
                   ("FWT", [D, D])]:
        d_in[n] = nc.dram_tensor(n, shp, F32, kind="ExternalInput").ap()
    for n, shp in [("CLa", [128, T]), ("SLa", [128, T]), ("CLb", [128, T]), ("SLb", [128, T]),
                   ("IDC", [128, 128]), ("MSK", [128, 128])]:
        d_in[n] = nc.dram_tensor(n, shp, F32, kind="ExternalInput").ap()
    for n, shp in [("lq1", [1, 32]), ("lk1", [1, 32]), ("lq2", [1, 32]), ("lk2", [1, 32]),
                   ("rmsw", [1, 64]), ("gam", [D]), ("bet", [D])]:
        d_in[n] = nc.dram_tensor(n, shp if isinstance(shp[0], int) and len(shp) > 1 else [1, shp[0]], F32, kind="ExternalInput").ap()
    out_d = nc.dram_tensor("outT", [D, T], F32, kind="ExternalOutput").ap()
    wscr = nc.dram_tensor("wscr", [D, D], BF16)
    cc_in = nc.dram_tensor("cc_in", [1, 8], F32)
    cc_out = nc.dram_tensor("cc_out", [1, 8], F32)

    from contextlib import ExitStack
    es = ExitStack()
    with es:
        sb = lambda n, shp, dt=BF16: es.enter_context(nc.sbuf_tensor(n, shp, dt))
        XT = sb("XT", [128, NC * T], F32)
        FS = sb("FS", [128, T], F32)
        ROWS = sb("ROWS", [1, 4096], F32)
        AXR = sb("AXR", [1, 400], F32)
        AUX = sb("AUX", [128, 64], F32)
        XLN = sb("XLN", [128, NC * T])
        CTXT = sb("CTXT", [128, NC * T])
        TABC = sb("TABC", [128, T])
        TABS = sb("TABS", [128, T])
        QB = sb("QB", [128, T])
        KB = sb("KB", [128, 2 * T])
        VB = sb("VB", [128, NT * 256])
        PT = sb("PT", [128, 2 * 1024])
        SQB = sb("SQB", [128, T])
        RSC = sb("RSC", [128, T])
        WA = sb("WA", [128, 4096])
        ONE2 = sb("ONE2", [128, 2])
        ONE4 = sb("ONE4", [128, 4])
        ONEB = sb("ONEB", [128, 1])
        ONER = sb("ONER", [1, 128], F32)
        CST = sb("CST", [128, 16], F32)
        IDB = sb("IDB", [128, 128])
        IDF = sb("IDF", [128, 128], F32)
        MSKB = sb("MSKB", [128, 128])
        ICST = sb("ICST", [128, 512], I32)
        CCS = sb("CCS", [1, 8], F32)
        psA = es.enter_context(nc.psum_tensor("psA", [128, 1024], F32))
        psB = es.enter_context(nc.psum_tensor("psB", [128, 1024], F32))
        psCa = es.enter_context(nc.psum_tensor("psCa", [128, 1024], F32))
        psCb = es.enter_context(nc.psum_tensor("psCb", [128, 1024], F32))
        SEM = {}
        for s in ("PE", "ACTT", "DVE", "LD", "LDW", "ST", "GP", "CC"):
            SEM[s] = es.enter_context(nc.semaphore(s))

        _cvals = [EPS_LN, EPS_RMS, -0.5, -1.0, 1.0, 1.0 / 64.0, SCALE_A, SCALE_B, 0.0]
        for _i, _v in enumerate(_cvals):
            nc.const_aps.register(F32, _v, CST[:, _i:_i + 1])

        Q = {k: [] for k in ("sp", "pe", "act", "dve", "gp")}
        N = {k: 0 for k in SEM}
        last_wait = {(e, s): 0 for e in Q for s in SEM}

        def put(eng, fn):
            Q[eng].append(fn)

        def tag(eng, fn, sem, by=1):
            N[sem] += by
            tok = N[sem]
            put(eng, lambda e, fn=fn, s=sem, b=by: fn(e).then_inc(SEM[s], b))
            return tok

        def wait(eng, sem, tok):
            if tok is None or tok <= 0:
                return
            if last_wait[(eng, sem)] >= tok:
                return
            last_wait[(eng, sem)] = tok
            put(eng, lambda e, s=sem, t=tok: e.wait_ge(SEM[s], t))

        def seg512(a, b):
            out = []
            while a < b:
                nb = min(b, (a // 512 + 1) * 512)
                out.append((a, nb))
                a = nb
            return out

        ldx = None
        for c in range(NC):
            ldx = tag("sp", lambda e, c=c: e.dma_start(
                out=XT[:, c * T:(c + 1) * T], in_=d_in["xT"][c * 128:(c + 1) * 128, :]),
                "LD", 16)
        ldc = None
        for nm, dst in [("CLa", TABC), ("SLa", TABS)]:
            ldc = tag("gp", lambda e, nm=nm, dst=dst: e.dma_start(out=dst[:, :], in_=d_in[nm]), "GP", 16)
        for nm, dst in [("IDC", IDB), ("MSK", MSKB)]:
            ldc = tag("gp", lambda e, nm=nm, dst=dst: e.dma_start(out=dst[:, :], in_=d_in[nm]), "GP", 16)
        ldf = tag("sp", lambda e: e.dma_start(out=IDF[:, :], in_=d_in["IDC"]), "LD", 16)
        ldg = tag("sp", lambda e: e.dma_start(
            out=AUX[:, 0:NC], in_=d_in["gam"].rearrange("o (c p) -> (o p) c", p=128),
            allow_slow_non_contiguous=True), "LD", 16)
        ldg = tag("sp", lambda e: e.dma_start(
            out=AUX[:, 8:8 + NC], in_=d_in["bet"].rearrange("o (c p) -> (o p) c", p=128),
            allow_slow_non_contiguous=True), "LD", 16)
        ldl = None
        for i, nm in enumerate(["lq1", "lk1", "lq2", "lk2"]):
            ldl = tag("sp", lambda e, i=i, nm=nm: e.dma_start(out=ROWS[0:1, 3072 + 32 * i:3104 + 32 * i], in_=d_in[nm]), "LD", 16)
        ldl = tag("sp", lambda e: e.dma_start(out=AXR[0:1, 64:128], in_=d_in["rmsw"]), "LD", 16)

        def dve_chain(fn):
            wait("dve", "DVE", N["DVE"])
            return tag("dve", fn, "DVE")
        for _i, _v in enumerate(_cvals):
            dve_chain(lambda e, _i=_i, _v=_v: e.memset(CST[:, _i:_i + 1], _v))
        dve_chain(lambda e: e.memset(ONER[:, :], 1.0))
        dve_chain(lambda e: e.memset(ONEB[:, :], 1.0))
        dve_chain(lambda e: e.memset(ONE2[:, :], 0.0))
        dve_chain(lambda e: e.memset(ONE2[0:64, 0:1], 1.0))
        dve_chain(lambda e: e.memset(ONE2[64:128, 1:2], 1.0))
        dve_chain(lambda e: e.memset(ONE4[:, :], 0.0))
        for j in range(4):
            dve_chain(lambda e, j=j: e.memset(ONE4[32 * j:32 * j + 32, j:j + 1], 1.0))
        dve_chain(lambda e: e.memset(AXR[0:1, 220:228], 0.0))
        vbones = dve_chain(lambda e: e.memset(
            VB[:, :].rearrange("p (b k) -> p b k", k=128)[:, :, 64:128], 1.0))

        wait("dve", "LD", ldl)
        t1 = tag("dve", lambda e: e.tensor_tensor(AXR[0:1, 128:160], ROWS[0:1, 3072:3104], ROWS[0:1, 3104:3136], op=ALU.mult), "DVE")
        t2 = tag("dve", lambda e: e.tensor_tensor(AXR[0:1, 160:192], ROWS[0:1, 3136:3168], ROWS[0:1, 3168:3200], op=ALU.mult), "DVE")
        wait("dve", "DVE", t2)
        t3 = tag("dve", lambda e: e.reduce_sum(AXR[0:1, 192:193], AXR[0:1, 128:160], axis=AX), "DVE")
        t4 = tag("dve", lambda e: e.reduce_sum(AXR[0:1, 193:194], AXR[0:1, 160:192], axis=AX), "DVE")
        wait("act", "DVE", t4)
        t5 = tag("act", lambda e: e.activation(AXR[0:1, 194:195], AXR[0:1, 192:193], ACT.Exp), "ACTT")
        t6 = tag("act", lambda e: e.activation(AXR[0:1, 195:196], AXR[0:1, 193:194], ACT.Exp), "ACTT")
        wait("dve", "ACTT", t6)
        t7 = tag("dve", lambda e: e.tensor_sub(AXR[0:1, 196:197], AXR[0:1, 194:195], AXR[0:1, 195:196]), "DVE")
        wait("dve", "DVE", t7)
        t8 = tag("dve", lambda e: e.tensor_scalar(AXR[0:1, 197:198], AXR[0:1, 196:197], 1.0, -LAMBDA_INIT, op0=ALU.mult, op1=ALU.subtract), "DVE")
        wait("dve", "DVE", t8)
        t8b = tag("dve", lambda e: e.tensor_scalar_mul(AXR[0:1, 197:198], AXR[0:1, 197:198], -1.0), "DVE")
        wait("dve", "DVE", t8b)
        t9 = tag("dve", lambda e: e.tensor_scalar_add(AXR[0:1, 198:199], AXR[0:1, 196:197], LAMBDA_INIT), "DVE")
        wait("dve", "DVE", t9)
        t10 = tag("dve", lambda e: e.tensor_scalar_mul(AXR[0:1, 199:200], AXR[0:1, 198:199], -1.0), "DVE")
        t11 = tag("dve", lambda e: e.tensor_scalar_mul(AXR[0:1, 256:320], AXR[0:1, 64:128], 1.0 - LAMBDA_INIT), "DVE")
        wait("pe", "DVE", t10)
        p1 = tag("pe", lambda e: e.matmul(psA[:, 0:1], ONER[0:1, :], AXR[0:1, 199:200], start=True, stop=True), "PE")
        wait("dve", "PE", p1)
        t12 = tag("dve", lambda e: e.tensor_copy(AUX[:, 38:39], psA[:, 0:1]), "DVE")

        put("dve", lambda e: e.memset(AUX[:, 41:44], 0.0))
        wsum_tok = None
        ld_tok = {}
        for c in range(NC):
            wait("sp", "DVE", N["DVE"])
            ld_tok[c] = tag("sp", lambda e, c=c: e.dma_start(
                out=FS[:, 0:1024],
                in_=d_in["FWT"][c * 128:(c + 1) * 128, :]), "LDW", 16)
            wait("dve", "LDW", ld_tok[c])
            wait("act", "LDW", ld_tok[c])
            wait("act", "DVE", N["DVE"])
            a1 = tag("act", lambda e: e.activation(FS[:, 1024:2048], FS[:, 0:1024], ACT.Abs), "ACTT")
            wait("dve", "ACTT", a1)
            a2 = tag("dve", lambda e: e.reduce_sum(AUX[:, 45:46], FS[:, 1024:2048], axis=AX), "DVE")
            wait("dve", "DVE", a2)
            wsum_tok = tag("dve", lambda e: e.tensor_add(AUX[:, 41:42], AUX[:, 41:42], AUX[:, 45:46]), "DVE")
            wait("dve", "DVE", wsum_tok)
        wait("pe", "DVE", wsum_tok)
        pt = tag("pe", lambda e: e.transpose(psA[0:1, 0:128], AUX[:, 41:42], IDF[:, :]), "PE")
        wait("dve", "PE", pt)
        dsum = tag("dve", lambda e: e.reduce_sum(AXR[0:1, 200:201], psA[0:1, 0:128], axis=AX), "DVE")
        wait("dve", "DVE", dsum)
        ddel = tag("dve", lambda e: e.tensor_scalar_mul(AXR[0:1, 201:202], AXR[0:1, 200:201], 0.7 / (D * D)), "DVE")
        wait("pe", "DVE", ddel)
        pdel = tag("pe", lambda e: e.matmul(psA[:, 0:1], ONER[0:1, :], AXR[0:1, 201:202], start=True, stop=True), "PE")
        wait("dve", "PE", pdel)
        dcol = tag("dve", lambda e: e.tensor_copy(AUX[:, 46:47], psA[:, 0:1]), "DVE")
        wq_gp = None
        for c in range(NC):
            wait("sp", "DVE", N["DVE"])
            wait("sp", "ACTT", N["ACTT"])
            l2 = tag("sp", lambda e, c=c: e.dma_start(out=FS[:, 0:1024], in_=d_in["FWT"][c * 128:(c + 1) * 128, :]), "LDW", 16)
            wait("dve", "LDW", l2)
            wait("dve", "DVE", dcol)
            if wq_gp is not None:
                wait("dve", "GP", wq_gp)
            wait("act", "LDW", l2)
            wait("act", "DVE", N["DVE"])
            b1 = tag("act", lambda e: e.activation(FS[:, 1024:2048], FS[:, 0:1024], ACT.Abs), "ACTT")
            wait("dve", "ACTT", b1)
            b2 = tag("dve", lambda e: e.tensor_single_scalar(RSC[:, 0:1024], FS[:, 1024:2048], AUX[:, 46:47], op=ALU.is_gt), "DVE")
            wait("act", "LDW", l2)
            wait("act", "GP", wq_gp if wq_gp else 0)
            b3 = tag("act", lambda e: e.activation(SQB[:, 0:1024], FS[:, 0:1024], ACT.Sign), "ACTT")
            wait("dve", "ACTT", b3)
            wait("dve", "DVE", b2)
            b4 = tag("dve", lambda e: e.tensor_tensor(XLN[:, 0:1024], RSC[:, 0:1024], SQB[:, 0:1024], op=ALU.mult), "DVE")
            b5 = tag("dve", lambda e: e.tensor_tensor(FS[:, 1024:2048], FS[:, 1024:2048], RSC[:, 0:1024], op=ALU.mult), "DVE")
            wait("dve", "DVE", b5)
            b6 = tag("dve", lambda e: e.reduce_sum(AUX[:, 45:46], FS[:, 1024:2048], axis=AX), "DVE")
            wait("dve", "DVE", b6)
            b7 = tag("dve", lambda e: e.tensor_add(AUX[:, 42:43], AUX[:, 42:43], AUX[:, 45:46]), "DVE")
            b8 = tag("dve", lambda e: e.reduce_sum(AUX[:, 47:48], RSC[:, 0:1024], axis=AX), "DVE")
            wait("dve", "DVE", b8)
            b9 = tag("dve", lambda e: e.tensor_add(AUX[:, 43:44], AUX[:, 43:44], AUX[:, 47:48]), "DVE")
            wait("dve", "DVE", b9)
            wait("gp", "DVE", b4)
            wq_gp = tag("gp", lambda e, c=c: e.dma_start(
                out=wscr.ap()[c * 128:(c + 1) * 128, :], in_=XLN[:, 0:1024]), "GP", 16)
        wait("pe", "DVE", b9)
        pa = tag("pe", lambda e: e.transpose(psA[0:1, 0:128], AUX[:, 42:43], IDF[:, :]), "PE")
        pb = tag("pe", lambda e: e.transpose(psB[0:1, 0:128], AUX[:, 43:44], IDF[:, :]), "PE")
        wait("dve", "PE", pb)
        c1 = tag("dve", lambda e: e.reduce_sum(AXR[0:1, 202:203], psA[0:1, 0:128], axis=AX), "DVE")
        c2 = tag("dve", lambda e: e.reduce_sum(AXR[0:1, 203:204], psB[0:1, 0:128], axis=AX), "DVE")
        wait("dve", "DVE", c2)
        c3 = tag("dve", lambda e: e.tensor_scalar_max(AXR[0:1, 203:204], AXR[0:1, 203:204], 1.0), "DVE")
        wait("dve", "DVE", c3)
        c4 = tag("dve", lambda e: e.reciprocal(AXR[0:1, 204:205], AXR[0:1, 203:204]), "DVE")
        wait("dve", "DVE", c4)
        alpha_tok = tag("dve", lambda e: e.tensor_tensor(AXR[0:1, 205:206], AXR[0:1, 202:203], AXR[0:1, 204:205], op=ALU.mult), "DVE")

        def emit_ln(dep_dve=None, dep_pe=None):
            wait("act", "DVE", N["DVE"]); wait("act", "PE", N["PE"]); wait("act", "LD", N["LD"])
            wait("dve", "PE", N["PE"]); wait("dve", "ACTT", N["ACTT"]); wait("dve", "GP", N["GP"])
            wait("pe", "DVE", N["DVE"]); wait("pe", "ACTT", N["ACTT"])
            pe_tok = None
            for c in range(NC):
                if pe_tok is not None:
                    wait("act", "PE", pe_tok)
                    wait("dve", "PE", pe_tok)
                a = tag("act", lambda e, c=c: e.copy(SQB[:, :], XT[:, c * T:(c + 1) * T]), "ACTT")
                wait("dve", "ACTT", a)
                sq = tag("dve", lambda e: e.tensor_tensor(RSC[:, :], SQB[:, :], SQB[:, :], op=ALU.mult), "DVE")
                wait("pe", "ACTT", a)
                wait("pe", "DVE", sq)
                for hf in range(2):
                    ps = psA if hf == 0 else psB
                    ps2 = psCa if hf == 0 else psCb
                    for sgi, (s0, s1) in enumerate(seg512(0, 1024)):
                        pe_tok = tag("pe", lambda e, ps=ps, hf=hf, s0=s0, s1=s1, c=c: e.matmul(
                            ps[0:1, s0:s1], ONEB[:, 0:1], SQB[:, hf * 1024 + s0:hf * 1024 + s1],
                            start=(c == 0), stop=(c == NC - 1)), "PE")
                        pe_tok = tag("pe", lambda e, ps2=ps2, hf=hf, s0=s0, s1=s1, c=c: e.matmul(
                            ps2[0:1, s0:s1], ONEB[:, 0:1], RSC[:, hf * 1024 + s0:hf * 1024 + s1],
                            start=(c == 0), stop=(c == NC - 1)), "PE")
            wait("dve", "PE", pe_tok)
            r = None
            for hf in range(2):
                ps = psA if hf == 0 else psB
                ps2 = psCa if hf == 0 else psCb
                r = tag("dve", lambda e, ps=ps, hf=hf: e.tensor_scalar_mul(ROWS[0:1, 2048 + hf * 1024:2048 + hf * 1024 + 1024], ps[0:1, :], 1.0 / D), "DVE")
                wait("dve", "DVE", r)
                r = tag("dve", lambda e, hf=hf: e.tensor_tensor(FS[0:1, hf * 1024:hf * 1024 + 1024], ROWS[0:1, 2048 + hf * 1024:2048 + hf * 1024 + 1024], ROWS[0:1, 2048 + hf * 1024:2048 + hf * 1024 + 1024], op=ALU.mult), "DVE")
                wait("dve", "DVE", r)
                r = tag("dve", lambda e, ps2=ps2, hf=hf: e.scalar_tensor_tensor(FS[0:1, hf * 1024:hf * 1024 + 1024], ps2[0:1, :], 1.0 / D, FS[0:1, hf * 1024:hf * 1024 + 1024], op0=ALU.mult, op1=ALU.subtract), "DVE")
            wait("act", "DVE", r)
            r = tag("act", lambda e: e.activation(FS[0:1, 0:T], FS[0:1, 0:T], ACT.Ln, bias=EPS_LN, scale=1.0), "ACTT")
            wait("act", "ACTT", r)
            r = tag("act", lambda e: e.activation(ROWS[0:1, 0:T], FS[0:1, 0:T], ACT.Exp, scale=-0.5), "ACTT")
            wait("dve", "ACTT", r)
            r2 = tag("dve", lambda e: e.tensor_tensor(ROWS[0:1, 2048:2048 + T], ROWS[0:1, 2048:2048 + T], ROWS[0:1, 0:T], op=ALU.mult), "DVE")
            wait("pe", "ACTT", r)
            wait("pe", "DVE", r2)
            ptok = None
            for hf in range(2):
                ps1 = psA if hf == 0 else psB
                ps2 = psCa if hf == 0 else psCb
                for (s0, s1) in seg512(0, 1024):
                    ptok = tag("pe", lambda e, ps1=ps1, hf=hf, s0=s0, s1=s1: e.matmul(
                        ps1[:, s0:s1], ONER[0:1, :], ROWS[0:1, hf * 1024 + s0:hf * 1024 + s1],
                        start=True, stop=True), "PE")
                    ptok = tag("pe", lambda e, ps2=ps2, hf=hf, s0=s0, s1=s1: e.matmul(
                        ps2[:, s0:s1], ONER[0:1, :], ROWS[0:1, 2048 + hf * 1024 + s0:2048 + hf * 1024 + s1],
                        start=True, stop=True), "PE")
            wait("dve", "PE", ptok)
            dtok = None
            for c in range(NC):
                for hf in range(2):
                    ps1 = psA if hf == 0 else psB
                    ps2 = psCa if hf == 0 else psCb
                    u = tag("dve", lambda e, c=c, hf=hf, ps1=ps1: e.tensor_tensor(
                        FS[:, hf * 1024:hf * 1024 + 1024], XT[:, c * T + hf * 1024:c * T + hf * 1024 + 1024],
                        ps1[:, :], op=ALU.mult), "DVE")
                    wait("dve", "DVE", u)
                    dtok = tag("dve", lambda e, c=c, hf=hf, ps2=ps2: e.tensor_tensor(
                        XLN[:, c * T + hf * 1024:c * T + hf * 1024 + 1024],
                        FS[:, hf * 1024:hf * 1024 + 1024], ps2[:, :], op=ALU.subtract), "DVE")
            return dtok

        def emit_rope(src_ps, rows, dst, dst_rows, shift, half_tok):
            r0, r1 = rows
            d0, d1 = dst_rows
            wait("act", "DVE", N["DVE"])
            wait("act", "PE", N["PE"])
            wait("dve", "PE", N["PE"])
            wait("dve", "GP", N["GP"])
            a1 = tag("act", lambda e: e.copy(RSC[r0:r1, 0:1024], psA[r0:r1, :]), "ACTT")
            a2 = tag("act", lambda e: e.copy(RSC[r0:r1, 1024:2048], psB[r0:r1, :]), "ACTT")
            wait("dve", "ACTT", a2)
            mask = [(i ^ 1) for i in range(32)] if shift == 1 else [(i + 16) % 32 for i in range(32)]
            g2 = tag("dve", lambda e, mask=mask: e.stream_shuffle(SQB[r0:r1, :], RSC[r0:r1, :], mask), "DVE")
            wait("dve", "ACTT", a2)
            wait("dve", "DVE", g2)
            v1 = tag("dve", lambda e: e.tensor_tensor(RSC[r0:r1, :],
                                                      RSC[r0:r1, :], TABC[r0:r1, :], op=ALU.mult), "DVE")
            wait("dve", "DVE", v1)
            v2 = tag("dve", lambda e: e.tensor_tensor(SQB[r0:r1, :], SQB[r0:r1, :], TABS[r0:r1, :], op=ALU.mult), "DVE")
            wait("dve", "DVE", v2)
            v3 = tag("dve", lambda e: e.tensor_tensor(dst[d0:d1, 0:T], RSC[r0:r1, :], SQB[r0:r1, :], op=ALU.add), "DVE")
            return v3

        def emit_norm_bound(qsrc, qrows_masks, ksrc, krows_masks, scale, aux_cols, ones_lhs):
            nheads = len(aux_cols)
            wait("dve", "PE", N["PE"]); wait("dve", "ACTT", N["ACTT"])
            wait("pe", "ACTT", N["ACTT"])
            sq = tag("dve", lambda e: e.tensor_tensor(RSC[:, :], qsrc[:, 0:T], qsrc[:, 0:T], op=ALU.mult), "DVE")
            wait("pe", "DVE", sq)
            ptok = None
            for hf in range(2):
                ps = psA if hf == 0 else psB
                for (s0, s1) in seg512(0, 1024):
                    ptok = tag("pe", lambda e, ps=ps, hf=hf, s0=s0, s1=s1: e.matmul(
                        ps[0:nheads, s0:s1], ones_lhs, RSC[:, hf * 1024 + s0:hf * 1024 + s1],
                        start=True, stop=True), "PE")
            sk = tag("dve", lambda e: e.tensor_tensor(SQB[:, :], ksrc[:, 0:T], ksrc[:, 0:T], op=ALU.mult), "DVE")
            wait("pe", "DVE", sk)
            ktok = None
            for hf in range(2):
                ps = psCa if hf == 0 else psCb
                for (s0, s1) in seg512(0, 1024):
                    ktok = tag("pe", lambda e, ps=ps, hf=hf, s0=s0, s1=s1: e.matmul(
                        ps[0:nheads, s0:s1], ones_lhs, SQB[:, hf * 1024 + s0:hf * 1024 + s1],
                        start=True, stop=True), "PE")
            wait("dve", "PE", ktok)
            d1 = tag("dve", lambda e: e.reduce_max(AXR[0:nheads, 0:1] if False else AUX[0:nheads, 48:49], psA[0:nheads, :], axis=AX), "DVE")
            d2 = tag("dve", lambda e: e.reduce_max(AUX[0:nheads, 49:50], psB[0:nheads, :], axis=AX), "DVE")
            d3 = tag("dve", lambda e: e.reduce_max(AUX[0:nheads, 50:51], psCa[0:nheads, :], axis=AX), "DVE")
            d4 = tag("dve", lambda e: e.reduce_max(AUX[0:nheads, 51:52], psCb[0:nheads, :], axis=AX), "DVE")
            wait("dve", "DVE", d4)
            d5 = tag("dve", lambda e: e.tensor_max(AUX[0:nheads, 48:49], AUX[0:nheads, 48:49], AUX[0:nheads, 49:50]), "DVE")
            d6 = tag("dve", lambda e: e.tensor_max(AUX[0:nheads, 50:51], AUX[0:nheads, 50:51], AUX[0:nheads, 51:52]), "DVE")
            wait("dve", "DVE", d6)
            d7 = tag("dve", lambda e: e.tensor_add(AUX[0:nheads, 48:49], AUX[0:nheads, 48:49], AUX[0:nheads, 50:51]), "DVE")
            wait("dve", "DVE", d7)
            d8 = tag("dve", lambda e: e.tensor_scalar_mul(AUX[0:nheads, 48:49], AUX[0:nheads, 48:49], -scale / 2.0), "DVE")
            wait("pe", "DVE", d8)
            tt = tag("pe", lambda e: e.transpose(psB[0:1, 0:nheads], AUX[0:nheads, 48:49], IDF[0:nheads, 0:nheads]), "PE")
            wait("dve", "PE", tt)
            m0 = tag("dve", lambda e: e.tensor_copy(AXR[0:1, 210:210 + nheads], psB[0:1, 0:nheads]), "DVE")
            toks = []
            for i in range(nheads):
                wait("pe", "DVE", m0)
                m2 = tag("pe", lambda e, i=i: e.matmul(psA[:, i:i + 1], ONER[0:1, :], AXR[0:1, 210 + i:211 + i], start=True, stop=True), "PE")
                wait("dve", "PE", m2)
                m3 = tag("dve", lambda e, i=i, col=aux_cols[i]: e.tensor_copy(AUX[:, col:col + 1], psA[:, i:i + 1]), "DVE")
                toks.append(m3)
            return toks[-1]

        lnt = emit_ln(dep_dve=alpha_tok)
        for p in range(8):
            wait("gp", "PE", N["PE"])
            wq = tag("gp", lambda e, p=p: e.dma_start(
                out=WA[:, 0:NC * 128].rearrange("q (c m) -> q c m", c=NC),
                in_=d_in["WQT"].rearrange("(c q) e -> q c e", q=128)[:, :, p * 128:(p + 1) * 128]), "GP", 16)
            wk = tag("gp", lambda e, p=p: e.dma_start(
                out=WA[:, NC * 128:2 * NC * 128].rearrange("q (c m) -> q c m", c=NC),
                in_=d_in["WKT"].rearrange("(c q) e -> q c e", q=128)[:, :, p * 128:(p + 1) * 128]), "GP", 16)
            wv = tag("gp", lambda e, p=p: e.dma_start(
                out=WA[:, 2 * NC * 128:3 * NC * 128].rearrange("q (c m) -> q c m", c=NC),
                in_=d_in["WVT"].rearrange("(c q) e -> q c e", q=128)[:, :, p * 128:(p + 1) * 128]), "GP", 16)
            wait("pe", "DVE", lnt)
            wait("pe", "GP", wv)
            qtok = None
            for c in range(NC):
                for hf in range(2):
                    ps = psA if hf == 0 else psB
                    for (s0, s1) in seg512(0, 1024):
                        qtok = tag("pe", lambda e, c=c, ps=ps, hf=hf, s0=s0, s1=s1: e.matmul(
                            ps[:, s0:s1], WA[:, c * 128:(c + 1) * 128],
                            XLN[:, c * T + hf * 1024 + s0:c * T + hf * 1024 + s1],
                            start=(c == 0), stop=(c == NC - 1)), "PE")
            wait("act", "PE", qtok)
            qv = emit_rope(None, (0, 128), QB, (0, 128), 1, None)
            wait("pe", "DVE", qv)
            ktok = None
            for c in range(NC):
                for hf in range(2):
                    ps = psA if hf == 0 else psB
                    for (s0, s1) in seg512(0, 1024):
                        ktok = tag("pe", lambda e, c=c, ps=ps, hf=hf, s0=s0, s1=s1: e.matmul(
                            ps[:, s0:s1], WA[:, NC * 128 + c * 128:NC * 128 + (c + 1) * 128],
                            XLN[:, c * T + hf * 1024 + s0:c * T + hf * 1024 + s1],
                            start=(c == 0), stop=(c == NC - 1)), "PE")
            wait("act", "PE", ktok)
            kv = emit_rope(None, (0, 128), KB, (0, 128), 1, None)
            wait("pe", "GP", wv)
            wait("pe", "DVE", vbones)
            vtoks = {}
            for i in range(NT):
                ps = psCa if i < 8 else psCb
                off = (i % 8) * 128
                vt = None
                for c in range(NC):
                    vt = tag("pe", lambda e, c=c, i=i, ps=ps, off=off: e.matmul(
                        ps[:, off:off + 128], XLN[:, c * T + i * 128:c * T + i * 128 + 128],
                        WA[:, 2 * NC * 128 + c * 128:2 * NC * 128 + (c + 1) * 128],
                        start=(c == 0), stop=(c == NC - 1)), "PE")
                vtoks[i] = vt
            va = None
            wait("act", "PE", vtoks[NT - 1])
            for i in range(NT):
                ps = psCa if i < 8 else psCb
                off = (i % 8) * 128
                va = tag("act", lambda e, i=i, ps=ps, off=off: e.copy(
                    VB[:, :].rearrange("p (b k) -> p b k", k=128)[:, 2 * i:2 * i + 2, 0:64],
                    ps[:, off:off + 128].rearrange("p (a k) -> p a k", a=2)), "ACTT")
            wait("dve", "DVE", qv); wait("dve", "DVE", kv); wait("dve", "ACTT", va)
            mb = emit_norm_bound(QB, None, KB, None, SCALE_A, [32, 33], ONE2[:, 0:2])
            wait("pe", "DVE", mb)
            wait("act", "DVE", mb)
            for hf in range(2):
                wait("pe", "DVE", N["DVE"])
                last_pv = {0: None, 1: None}
                exp_tok = {0: None, 1: None}
                for tk in range(NT):
                    stoks = {}
                    for hh in range(2):
                        ps = psA if hh == 0 else psB
                        if exp_tok[hh] is not None:
                            wait("pe", "ACTT", exp_tok[hh])
                        stok = None
                        for (s0, s1) in seg512(0, 1024):
                            stok = tag("pe", lambda e, ps=ps, hh=hh, tk=tk, hf=hf, s0=s0, s1=s1: e.matmul(
                                ps[:, s0:s1], KB[64 * hh:64 * hh + 64, tk * 128:(tk + 1) * 128],
                                QB[64 * hh:64 * hh + 64, hf * 1024 + s0:hf * 1024 + s1],
                                start=True, stop=True, tile_position=(64 * hh, 0)), "PE")
                        stoks[hh] = stok
                    for hh in range(2):
                        ps = psA if hh == 0 else psB
                        wait("act", "PE", stoks[hh])
                        exp_tok[hh] = tag("act", lambda e, ps=ps, hh=hh: e.activation(
                            PT[:, hh * 1024:(hh + 1) * 1024], ps[:, :], ACT.Exp,
                            bias=AUX[:, 32 + hh:33 + hh], scale=SCALE_A), "ACTT")
                        psc = psCa if hh == 0 else psCb
                        wait("pe", "ACTT", exp_tok[hh])
                        for (s0, s1) in seg512(0, 1024):
                            last_pv[hh] = tag("pe", lambda e, psc=psc, hh=hh, tk=tk, s0=s0, s1=s1: e.matmul(
                                psc[:, s0:s1],
                                VB[:, :].rearrange("p (b k) -> p b k", k=128)[:, 2 * tk + hh, :],
                                PT[:, hh * 1024 + s0:hh * 1024 + s1],
                                start=(tk == 0), stop=(tk == NT - 1)), "PE")
                for hh in range(2):
                    psc = psCa if hh == 0 else psCb
                    wait("dve", "PE", last_pv[hh])
                    wait("dve", "DVE", N["DVE"])
                    e1 = tag("dve", lambda e, psc=psc: e.reciprocal(FS[0:64, 0:1024], psc[64:128, :]), "DVE")
                    wait("dve", "DVE", e1)
                    e2 = tag("dve", lambda e, psc=psc, hh=hh, p=p, hf=hf: e.tensor_tensor(
                        CTXT[64 * hh:64 * hh + 64, p * T + hf * 1024:p * T + hf * 1024 + 1024],
                        psc[0:64, :], FS[0:64, 0:1024], op=ALU.mult), "DVE")
                    if hf == 0:
                        wait("pe", "DVE", e2)
            ctx_done = N["DVE"]
        wait("pe", "DVE", ctx_done)
        res_tok = None

        for eo in range(NC):
            wait("gp", "PE", N["PE"])
            wo = tag("gp", lambda e, eo=eo: e.dma_start(
                out=WA[:, 3 * NC * 128: 3 * NC * 128 + NC * 128].rearrange("q (c m) -> q c m", c=NC),
                in_=d_in["WOT"].rearrange("(c q) e -> q c e", q=128)[:, :, eo * 128:(eo + 1) * 128]), "GP", 16)
            wait("pe", "GP", wo)
            for hf in range(2):
                ps = psA if hf == 0 else psB
                if res_tok is not None:
                    wait("pe", "DVE", res_tok)
                otok = None
                for c in range(NC):
                    for (s0, s1) in seg512(0, 1024):
                        otok = tag("pe", lambda e, c=c, eo=eo, hf=hf, ps=ps, s0=s0, s1=s1: e.matmul(
                            ps[:, s0:s1], WA[:, 3 * NC * 128 + c * 128:3 * NC * 128 + (c + 1) * 128],
                            CTXT[:, c * T + hf * 1024 + s0:c * T + hf * 1024 + s1],
                            start=(c == 0), stop=(c == NC - 1)), "PE")
                wait("dve", "PE", otok)
                res_tok = tag("dve", lambda e, eo=eo, hf=hf, ps=ps: e.tensor_tensor(
                    XT[:, eo * T + hf * 1024:eo * T + hf * 1024 + 1024],
                    XT[:, eo * T + hf * 1024:eo * T + hf * 1024 + 1024], ps[:, :], op=ALU.add), "DVE")

        def emit_ffn(tag_id):
            wait("act", "DVE", N["DVE"]); wait("act", "PE", N["PE"])
            wait("dve", "PE", N["PE"]); wait("dve", "ACTT", N["ACTT"]); wait("dve", "GP", N["GP"])
            wait("pe", "DVE", N["DVE"]); wait("pe", "ACTT", N["ACTT"])
            wait("gp", "PE", N["PE"]); wait("gp", "DVE", N["DVE"])
            pe_tok = None
            for c in range(NC):
                if pe_tok is not None:
                    wait("act", "PE", pe_tok)
                    wait("dve", "PE", pe_tok)
                a = tag("act", lambda e, c=c: e.copy(SQB[:, :], XT[:, c * T:(c + 1) * T]), "ACTT")
                wait("dve", "ACTT", a)
                sq = tag("dve", lambda e: e.tensor_tensor(RSC[:, :], SQB[:, :], SQB[:, :], op=ALU.mult), "DVE")
                wait("pe", "ACTT", a); wait("pe", "DVE", sq)
                for hfq in range(2):
                    ps = psA if hfq == 0 else psB
                    ps2 = psCa if hfq == 0 else psCb
                    for (s0, s1) in seg512(0, 1024):
                        pe_tok = tag("pe", lambda e, ps=ps, hfq=hfq, s0=s0, s1=s1, c=c: e.matmul(
                            ps[0:1, s0:s1], ONEB[:, 0:1], SQB[:, hfq * 1024 + s0:hfq * 1024 + s1],
                            start=(c == 0), stop=(c == NC - 1)), "PE")
                        pe_tok = tag("pe", lambda e, ps2=ps2, hfq=hfq, s0=s0, s1=s1, c=c: e.matmul(
                            ps2[0:1, s0:s1], ONEB[:, 0:1], RSC[:, hfq * 1024 + s0:hfq * 1024 + s1],
                            start=(c == 0), stop=(c == NC - 1)), "PE")
            wait("dve", "PE", pe_tok)
            r2 = None
            for hfq in range(2):
                ps = psA if hfq == 0 else psB
                ps2 = psCa if hfq == 0 else psCb
                h0 = hfq * 1024
                h1 = (1 - hfq) * 1024
                r = tag("dve", lambda e, ps=ps, h0=h0: e.tensor_scalar_mul(ROWS[0:1, 2048 + h0:2048 + h0 + 1024], ps[0:1, :], 1.0 / D), "DVE")
                wait("dve", "DVE", r)
                r = tag("dve", lambda e, h0=h0: e.tensor_tensor(FS[0:1, h0:h0 + 1024], ROWS[0:1, 2048 + h0:2048 + h0 + 1024], ROWS[0:1, 2048 + h0:2048 + h0 + 1024], op=ALU.mult), "DVE")
                wait("dve", "DVE", r)
                r = tag("dve", lambda e, ps2=ps2, h0=h0: e.scalar_tensor_tensor(FS[0:1, h0:h0 + 1024], ps2[0:1, :], 1.0 / D, FS[0:1, h0:h0 + 1024], op0=ALU.mult, op1=ALU.subtract), "DVE")
                wait("act", "DVE", r)
                r = tag("act", lambda e, h0=h0, h1=h1: e.activation(FS[0:1, h1:h1 + 1024], FS[0:1, h0:h0 + 1024], ACT.Ln, bias=EPS_LN, scale=1.0), "ACTT")
                wait("act", "ACTT", r)
                r = tag("act", lambda e, h0=h0, h1=h1: e.activation(ROWS[0:1, h0:h0 + 1024], FS[0:1, h1:h1 + 1024], ACT.Exp, scale=-0.5), "ACTT")
                wait("act", "ACTT", r)
                r = tag("act", lambda e, h1=h1: e.activation(FS[0:1, h1:h1 + 1024], FS[0:1, h1:h1 + 1024], ACT.Exp, scale=-1.0), "ACTT")
                wait("dve", "ACTT", r)
                r = tag("dve", lambda e, h0=h0, h1=h1: e.tensor_tensor(FS[0:1, h0:h0 + 1024], FS[0:1, h0:h0 + 1024], FS[0:1, h1:h1 + 1024], op=ALU.mult), "DVE")
                wait("act", "DVE", r)
                r = tag("act", lambda e, h0=h0: e.activation(FS[0:1, h0:h0 + 1024], FS[0:1, h0:h0 + 1024], ACT.Ln, bias=EPS_LN, scale=1.0), "ACTT")
                wait("act", "ACTT", r)
                r = tag("act", lambda e, h0=h0, h1=h1: e.activation(FS[0:1, h1:h1 + 1024], FS[0:1, h0:h0 + 1024], ACT.Exp, scale=-0.5), "ACTT")
                wait("dve", "ACTT", r)
                r = tag("dve", lambda e, h0=h0, h1=h1: e.tensor_tensor(ROWS[0:1, h0:h0 + 1024], ROWS[0:1, h0:h0 + 1024], FS[0:1, h1:h1 + 1024], op=ALU.mult), "DVE")
                wait("dve", "DVE", r)
                r2 = tag("dve", lambda e, h0=h0: e.tensor_tensor(ROWS[0:1, 2048 + h0:2048 + h0 + 1024], ROWS[0:1, 2048 + h0:2048 + h0 + 1024], ROWS[0:1, h0:h0 + 1024], op=ALU.mult), "DVE")
            wait("pe", "DVE", r2)
            ptok = None
            for hfq in range(2):
                ps1 = psA if hfq == 0 else psB
                ps2 = psCa if hfq == 0 else psCb
                for (s0, s1) in seg512(0, 1024):
                    ptok = tag("pe", lambda e, ps1=ps1, hfq=hfq, s0=s0, s1=s1: e.matmul(
                        ps1[:, s0:s1], ONER[0:1, :], ROWS[0:1, hfq * 1024 + s0:hfq * 1024 + s1], start=True, stop=True), "PE")
                    ptok = tag("pe", lambda e, ps2=ps2, hfq=hfq, s0=s0, s1=s1: e.matmul(
                        ps2[:, s0:s1], ONER[0:1, :], ROWS[0:1, 2048 + hfq * 1024 + s0:2048 + hfq * 1024 + s1], start=True, stop=True), "PE")
            wait("dve", "PE", ptok)
            am = tag("dve", lambda e: e.memset(AUX[:, 52:53], 0.0), "DVE")
            for c in range(NC):
                for hfq in range(2):
                    ps1 = psA if hfq == 0 else psB
                    ps2 = psCa if hfq == 0 else psCb
                    wait("dve", "ACTT", N["ACTT"])
                    u1 = tag("dve", lambda e, c=c, hfq=hfq, ps1=ps1: e.tensor_tensor(
                        FS[:, hfq * 1024:hfq * 1024 + 1024],
                        XT[:, c * T + hfq * 1024:c * T + hfq * 1024 + 1024], ps1[:, :], op=ALU.mult), "DVE")
                    wait("dve", "DVE", u1)
                    u2 = tag("dve", lambda e, c=c, hfq=hfq, ps2=ps2: e.tensor_tensor(
                        FS[:, hfq * 1024:hfq * 1024 + 1024],
                        FS[:, hfq * 1024:hfq * 1024 + 1024], ps2[:, :], op=ALU.subtract), "DVE")
                    wait("dve", "DVE", u2)
                    u3 = tag("dve", lambda e, c=c, hfq=hfq: e.tensor_scalar(
                        FS[:, (1 - hfq) * 1024:(1 - hfq) * 1024 + 1024], FS[:, hfq * 1024:hfq * 1024 + 1024],
                        AUX[:, c:c + 1], AUX[:, 8 + c:9 + c], op0=ALU.mult, op1=ALU.add), "DVE")
                    wait("dve", "DVE", u3)
                    u4 = tag("dve", lambda e, hfq=hfq: e.reduce_max(
                        AUX[:, 53:54], FS[:, (1 - hfq) * 1024:(1 - hfq) * 1024 + 1024], axis=AX,
                        apply_absolute_value=True), "DVE")
                    wait("dve", "DVE", u4)
                    am = tag("dve", lambda e: e.tensor_max(AUX[:, 52:53], AUX[:, 52:53], AUX[:, 53:54]), "DVE")
                    wait("dve", "DVE", am)
            wait("pe", "DVE", am)
            pm = tag("pe", lambda e: e.transpose(psA[0:1, 0:128], AUX[:, 52:53], IDF[:, :]), "PE")
            wait("dve", "PE", pm)
            gm = tag("dve", lambda e: e.reduce_max(AXR[0:1, 220:221], psA[0:1, 0:128], axis=AX), "DVE")
            wait("dve", "DVE", gm)
            gm2 = tag("dve", lambda e: e.tensor_single_scalar(AXR[0:1, 220:221], AXR[0:1, 220:221], 1e-8, op=ALU.max), "DVE")
            wait("gp", "DVE", gm2)
            g1 = tag("gp", lambda e: e.dma_start(out=cc_in.ap(), in_=AXR[0:1, 220:228]), "GP", 16)
            wait("gp", "GP", g1)
            g2 = tag("gp", lambda e: e.collective_compute(
                "AllReduce", ALU.max, replica_groups=[list(range(NCORES))],
                ins=[cc_in.ap().opt()], outs=[cc_out.ap().opt()]), "CC")
            wait("gp", "CC", g2)
            g3 = tag("gp", lambda e: e.dma_start(out=CCS[:, :], in_=cc_out.ap()), "GP", 16)
            wait("dve", "GP", g3)
            i1 = tag("dve", lambda e: e.reciprocal(AXR[0:1, 221:222], CCS[0:1, 0:1]), "DVE")
            wait("dve", "DVE", i1)
            i2 = tag("dve", lambda e: e.tensor_scalar_mul(AXR[0:1, 221:222], AXR[0:1, 221:222], 127.0), "DVE")
            i3 = tag("dve", lambda e: e.tensor_scalar_mul(AXR[0:1, 222:223], CCS[0:1, 0:1], 1.0 / 127.0), "DVE")
            wait("dve", "DVE", i3)
            i4 = tag("dve", lambda e: e.tensor_tensor(AXR[0:1, 223:224], AXR[0:1, 222:223], AXR[0:1, 205:206], op=ALU.mult), "DVE")
            wait("pe", "DVE", i4)
            i5 = tag("pe", lambda e: e.matmul(psA[:, 0:1], ONER[0:1, :], AXR[0:1, 221:222], start=True, stop=True), "PE")
            i6 = tag("pe", lambda e: e.matmul(psA[:, 1:2], ONER[0:1, :], AXR[0:1, 223:224], start=True, stop=True), "PE")
            wait("dve", "PE", i6)
            i7 = tag("dve", lambda e: e.tensor_copy(AUX[:, 54:55], psA[:, 0:1]), "DVE")
            i8 = tag("dve", lambda e: e.tensor_copy(AUX[:, 55:56], psA[:, 1:2]), "DVE")
            wait("dve", "DVE", i8)
            gi = tag("dve", lambda e: e.tensor_scalar(AUX[:, 16:16 + NC], AUX[:, 0:NC], AUX[:, 54:55], None, op0=ALU.mult), "DVE")
            bi = tag("dve", lambda e: e.tensor_scalar(AUX[:, 24:24 + NC], AUX[:, 8:8 + NC], AUX[:, 54:55], None, op0=ALU.mult), "DVE")
            wait("dve", "DVE", bi)
            wait("pe", "DVE", bi)
            ptok = None
            for hfq in range(2):
                ps1 = psA if hfq == 0 else psB
                ps2 = psCa if hfq == 0 else psCb
                for (s0, s1) in seg512(0, 1024):
                    ptok = tag("pe", lambda e, ps1=ps1, hfq=hfq, s0=s0, s1=s1: e.matmul(
                        ps1[:, s0:s1], ONER[0:1, :], ROWS[0:1, hfq * 1024 + s0:hfq * 1024 + s1], start=True, stop=True), "PE")
                    ptok = tag("pe", lambda e, ps2=ps2, hfq=hfq, s0=s0, s1=s1: e.matmul(
                        ps2[:, s0:s1], ONER[0:1, :], ROWS[0:1, 2048 + hfq * 1024 + s0:2048 + hfq * 1024 + s1], start=True, stop=True), "PE")
            wait("dve", "PE", ptok)
            xq_tok = None
            for c in range(NC):
                for hfq in range(2):
                    ps1 = psA if hfq == 0 else psB
                    ps2 = psCa if hfq == 0 else psCb
                    wait("dve", "ACTT", N["ACTT"])
                    u1 = tag("dve", lambda e, c=c, hfq=hfq, ps1=ps1: e.tensor_tensor(
                        FS[:, hfq * 1024:hfq * 1024 + 1024],
                        XT[:, c * T + hfq * 1024:c * T + hfq * 1024 + 1024], ps1[:, :], op=ALU.mult), "DVE")
                    wait("dve", "DVE", u1)
                    u2 = tag("dve", lambda e, c=c, hfq=hfq, ps2=ps2: e.tensor_tensor(
                        FS[:, hfq * 1024:hfq * 1024 + 1024],
                        FS[:, hfq * 1024:hfq * 1024 + 1024], ps2[:, :], op=ALU.subtract), "DVE")
                    wait("dve", "DVE", u2)
                    u3 = tag("dve", lambda e, c=c, hfq=hfq: e.tensor_scalar(
                        FS[:, (1 - hfq) * 1024:(1 - hfq) * 1024 + 1024], FS[:, hfq * 1024:hfq * 1024 + 1024],
                        AUX[:, 16 + c:17 + c], AUX[:, 24 + c:25 + c], op0=ALU.mult, op1=ALU.add), "DVE")
                    wait("dve", "DVE", u3)
                    for qq in range(2):
                        if xq_tok is not None:
                            wait("dve", "DVE", xq_tok)
                        u4 = tag("dve", lambda e, hfq=hfq, qq=qq: e.tensor_copy(
                            ICST[:, :], FS[:, (1 - hfq) * 1024 + qq * 512:(1 - hfq) * 1024 + qq * 512 + 512]), "DVE")
                        wait("dve", "DVE", u4)
                        xq_tok = tag("dve", lambda e, c=c, hfq=hfq, qq=qq: e.tensor_copy(
                            XLN[:, c * T + hfq * 1024 + qq * 512:c * T + hfq * 1024 + qq * 512 + 512], ICST[:, :]), "DVE")
            res = None
            for ot in range(NC):
                wslot = 3 * NC * 128
                wait("gp", "PE", N["PE"])
                wt = tag("gp", lambda e, ot=ot, wslot=wslot: e.dma_start(
                    out=WA[:, wslot:wslot + NC * 128].rearrange("q (c m) -> q c m", c=NC),
                    in_=wscr.ap().rearrange("(c q) o -> q c o", q=128)[:, :, ot * 128:(ot + 1) * 128]), "GP", 16)
                wait("pe", "GP", wt)
                wait("pe", "DVE", xq_tok)
                for hfq in range(2):
                    ps = psA if hfq == 0 else psB
                    if res is not None:
                        wait("pe", "DVE", res)
                    mt = None
                    for c in range(NC):
                        for (s0, s1) in seg512(0, 1024):
                            mt = tag("pe", lambda e, c=c, ot=ot, hfq=hfq, ps=ps, s0=s0, s1=s1, wslot=wslot: e.matmul(
                                ps[:, s0:s1], WA[:, wslot + c * 128:wslot + (c + 1) * 128],
                                XLN[:, c * T + hfq * 1024 + s0:c * T + hfq * 1024 + s1],
                                start=(c == 0), stop=(c == NC - 1)), "PE")
                    wait("dve", "PE", mt)
                    res = tag("dve", lambda e, ot=ot, hfq=hfq, ps=ps: e.scalar_tensor_tensor(
                        XT[:, ot * T + hfq * 1024:ot * T + hfq * 1024 + 1024], ps[:, :],
                        AUX[:, 55:56], XT[:, ot * T + hfq * 1024:ot * T + hfq * 1024 + 1024],
                        op0=ALU.mult, op1=ALU.add), "DVE")
            return res

        ffn1 = emit_ffn(1)

        wait("gp", "DVE", N["DVE"]); wait("gp", "PE", N["PE"]); wait("gp", "ACTT", N["ACTT"])
        tb = tag("gp", lambda e: e.dma_start(out=TABC[:, :], in_=d_in["CLb"]), "GP", 16)
        tb = tag("gp", lambda e: e.dma_start(out=TABS[:, :], in_=d_in["SLb"]), "GP", 16)
        lnt = emit_ln(dep_dve=ffn1)
        kzero = tag("dve", lambda e: e.memset(KB[32:64, 0:T], 0.0), "DVE")
        kzero = tag("dve", lambda e: e.memset(KB[96:128, 0:T], 0.0), "DVE")
        kzero = tag("dve", lambda e: e.memset(KB[0:32, T:2 * T], 0.0), "DVE")
        kzero = tag("dve", lambda e: e.memset(KB[64:96, T:2 * T], 0.0), "DVE")
        for p in range(8):
            g = p
            wait("gp", "PE", N["PE"])
            wq = tag("gp", lambda e, p=p: e.dma_start(
                out=WA[:, 0:NC * 128].rearrange("q (c m) -> q c m", c=NC),
                in_=d_in["Wq"].rearrange("(c q) e -> q c e", q=128)[:, :, p * 128:(p + 1) * 128]), "GP", 16)
            wk = tag("gp", lambda e, g=g: e.dma_start(
                out=WA[:, NC * 128:NC * 128 + NC * 64].rearrange("q (c m) -> q c m", c=NC),
                in_=d_in["Wk"].rearrange("(c q) e -> q c e", q=128)[:, :, g * 64:(g + 1) * 64]), "GP", 16)
            wk = tag("gp", lambda e, g=g: e.dma_start(
                out=WA[:, NC * 128 + NC * 64:NC * 128 + NC * 128].rearrange("q (c m) -> q c m", c=NC),
                in_=d_in["Wk"].rearrange("(c q) e -> q c e", q=128)[:, :, g * 64:(g + 1) * 64]), "GP", 16)
            wv = tag("gp", lambda e, g=g: e.dma_start(
                out=WA[:, 2 * NC * 128:2 * NC * 128 + NC * 64].rearrange("q (c m) -> q c m", c=NC),
                in_=d_in["Wv"].rearrange("(c q) e -> q c e", q=128)[:, :, g * 64:(g + 1) * 64]), "GP", 16)
            wait("pe", "DVE", lnt)
            wait("pe", "GP", wv)
            wait("pe", "GP", wk)
            wait("pe", "GP", tb)
            qtok = None
            for c in range(NC):
                for hfq in range(2):
                    ps = psA if hfq == 0 else psB
                    for (s0, s1) in seg512(0, 1024):
                        qtok = tag("pe", lambda e, c=c, ps=ps, hfq=hfq, s0=s0, s1=s1: e.matmul(
                            ps[:, s0:s1], WA[:, c * 128:(c + 1) * 128],
                            XLN[:, c * T + hfq * 1024 + s0:c * T + hfq * 1024 + s1],
                            start=(c == 0), stop=(c == NC - 1)), "PE")
            wait("act", "PE", qtok)
            qv = emit_rope(None, (0, 128), QB, (0, 128), 16, None)
            wait("pe", "DVE", qv)
            ktok = None
            for c in range(NC):
                for hfq in range(2):
                    ps = psA if hfq == 0 else psB
                    for (s0, s1) in seg512(0, 1024):
                        ktok = tag("pe", lambda e, c=c, ps=ps, hfq=hfq, s0=s0, s1=s1: e.matmul(
                            ps[0:64, s0:s1], WA[:, NC * 128 + c * 64:NC * 128 + (c + 1) * 64],
                            XLN[:, c * T + hfq * 1024 + s0:c * T + hfq * 1024 + s1],
                            start=(c == 0), stop=(c == NC - 1)), "PE")
            wait("act", "PE", ktok)
            kv = emit_rope(None, (0, 64), RSC, (0, 64), 16, None)
            wait("act", "DVE", kv)
            wait("act", "DVE", kzero)
            wait("act", "PE", N["PE"])
            wait("gp", "DVE", kv)
            wait("gp", "PE", N["PE"])
            ka = tag("act", lambda e: e.copy(KB[0:32, 0:T], RSC[0:32, :]), "ACTT")
            ka = tag("act", lambda e: e.copy(KB[32:64, T:2 * T], RSC[32:64, :]), "ACTT")
            kg = tag("gp", lambda e: e.dma_start(out=KB[64:96, 0:T], in_=RSC[0:32, :]), "GP", 16)
            kg = tag("gp", lambda e: e.dma_start(out=KB[96:128, T:2 * T], in_=RSC[32:64, :]), "GP", 16)
            kb2 = ka
            wait("pe", "GP", kg)
            wait("dve", "GP", kg)
            wait("pe", "GP", wv)
            wait("pe", "DVE", vbones)
            vtoks = {}
            for i in range(NT):
                ps = psCa if i < 8 else psCb
                off = (i % 8) * 128
                vt = None
                for c in range(NC):
                    vt = tag("pe", lambda e, c=c, i=i, ps=ps, off=off: e.matmul(
                        ps[:, off:off + 64], XLN[:, c * T + i * 128:c * T + i * 128 + 128],
                        WA[:, 2 * NC * 128 + c * 64:2 * NC * 128 + (c + 1) * 64],
                        start=(c == 0), stop=(c == NC - 1)), "PE")
                vtoks[i] = vt
            va = None
            wait("act", "PE", vtoks[NT - 1])
            for i in range(NT):
                ps = psCa if i < 8 else psCb
                off = (i % 8) * 128
                va = tag("act", lambda e, i=i, ps=ps, off=off: e.copy(
                    VB[:, :].rearrange("p (b k) -> p b k", k=128)[:, i, 0:64], ps[:, off:off + 64]), "ACTT")
            wait("dve", "DVE", qv); wait("dve", "ACTT", kb2); wait("dve", "ACTT", va)
            mb = emit_norm_bound(QB, None, KB, None, SCALE_B, [34, 35, 36, 37], ONE4[:, 0:4])
            wait("pe", "DVE", mb)
            wait("act", "DVE", mb)
            for pair in range(2):
                qoff = 64 * pair
                for hfq in range(2):
                    wait("pe", "DVE", N["DVE"])
                    t0 = hfq * 1024
                    t1 = t0 + 1024
                    ntk = (t1 + 127) // 128
                    last_pv = {0: None, 1: None}
                    exp_tok = {0: None, 1: None}
                    for tk in range(ntk):
                        trim = max(t0, tk * 128)
                        if trim >= t1:
                            continue
                        stoks = {}
                        for hh in range(2):
                            ps = psA if hh == 0 else psB
                            if exp_tok[hh] is not None:
                                wait("pe", "ACTT", exp_tok[hh])
                            stok = None
                            for (s0, s1) in seg512(trim - t0, 1024):
                                stok = tag("pe", lambda e, ps=ps, hh=hh, tk=tk, t0=t0, s0=s0, s1=s1, qoff=qoff: e.matmul(
                                    ps[:, s0:s1], KB[qoff:qoff + 64, hh * T + tk * 128:hh * T + (tk + 1) * 128],
                                    QB[qoff:qoff + 64, t0 + s0:t0 + s1],
                                    start=True, stop=True), "PE")
                            if 128 * tk >= t0:
                                stok = tag("pe", lambda e, ps=ps, tk=tk, t0=t0: e.matmul(
                                    ps[:, tk * 128 - t0:tk * 128 - t0 + 128], MSKB[:, :], IDB[:, :],
                                    start=False, stop=True, skip_group_check=True), "PE")
                            stoks[hh] = stok
                        for hh in range(2):
                            ps = psA if hh == 0 else psB
                            wait("act", "PE", stoks[hh])
                            mcol = 34 + 2 * pair + hh
                            exp_tok[hh] = tag("act", lambda e, ps=ps, hh=hh, trim=trim, t0=t0, mcol=mcol: e.activation(
                                PT[:, hh * 1024 + trim - t0:(hh + 1) * 1024], ps[:, trim - t0:1024], ACT.Exp,
                                bias=AUX[:, mcol:mcol + 1], scale=SCALE_B), "ACTT")
                            mz = None
                            if trim > t0:
                                if last_pv[hh] is not None:
                                    wait("dve", "PE", last_pv[hh])
                                mz = tag("dve", lambda e, hh=hh, trim=trim, t0=t0: e.memset(
                                    PT[:, hh * 1024:hh * 1024 + trim - t0], 0.0), "DVE")
                            psc = psCa if hh == 0 else psCb
                            wait("pe", "ACTT", exp_tok[hh])
                            if mz is not None:
                                wait("pe", "DVE", mz)
                            for (s0, s1) in seg512(0, 1024):
                                last_pv[hh] = tag("pe", lambda e, psc=psc, hh=hh, tk=tk, s0=s0, s1=s1, ntk=ntk: e.matmul(
                                    psc[:, s0:s1],
                                    VB[:, :].rearrange("p (b k) -> p b k", k=128)[:, tk, :],
                                    PT[:, hh * 1024 + s0:hh * 1024 + s1],
                                    start=(tk == 0), stop=(tk == ntk - 1)), "PE")
                    wait("dve", "PE", last_pv[0]); wait("dve", "PE", last_pv[1])
                    e1 = tag("dve", lambda e: e.reciprocal(FS[0:64, 0:1024], psCa[64:128, :]), "DVE")
                    wait("dve", "DVE", e1)
                    e2 = tag("dve", lambda e: e.tensor_tensor(FS[0:64, 1024:2048], psCa[0:64, :], FS[0:64, 0:1024], op=ALU.mult), "DVE")
                    e3 = tag("dve", lambda e: e.reciprocal(FS[64:128, 0:1024], psCb[64:128, :]), "DVE")
                    wait("dve", "DVE", e3)
                    e4 = tag("dve", lambda e: e.tensor_tensor(FS[0:64, 0:1024], psCb[0:64, :], FS[64:128, 0:1024], op=ALU.mult), "DVE")
                    wait("dve", "DVE", e4)
                    e5 = tag("dve", lambda e: e.scalar_tensor_tensor(
                        FS[0:64, 0:1024], FS[0:64, 0:1024], AUX[0:64, 38:39], FS[0:64, 1024:2048],
                        op0=ALU.mult, op1=ALU.add), "DVE")
                    wait("dve", "DVE", e5)
                    e6 = tag("dve", lambda e: e.tensor_tensor(SQB[0:64, 0:1024], FS[0:64, 0:1024], FS[0:64, 0:1024], op=ALU.mult), "DVE")
                    wait("pe", "DVE", e6)
                    rtok = None
                    for (s0, s1) in seg512(0, 1024):
                        rtok = tag("pe", lambda e, s0=s0, s1=s1: e.matmul(
                            psA[0:1, s0:s1], ONEB[0:64, 0:1], SQB[0:64, s0:s1], start=True, stop=True), "PE")
                    wait("act", "PE", rtok)
                    r1 = tag("act", lambda e: e.activation(ROWS[0:1, 3072:4096], psA[0:1, :], ACT.Ln, bias=EPS_RMS, scale=1.0 / 64.0), "ACTT")
                    wait("act", "ACTT", r1)
                    r2 = tag("act", lambda e: e.activation(ROWS[0:1, 2048:3072], ROWS[0:1, 3072:4096], ACT.Exp, scale=-0.5), "ACTT")
                    wait("pe", "ACTT", r2)
                    rep = None
                    for (s0, s1) in seg512(0, 1024):
                        rep = tag("pe", lambda e, s0=s0, s1=s1: e.matmul(
                            psB[0:64, s0:s1], AXR[0:1, 256:320], ROWS[0:1, 2048 + s0:2048 + s1], start=True, stop=True), "PE")
                    wait("dve", "PE", rep)
                    h = 2 * p + pair
                    crow = 64 * (h % 2)
                    cch = h // 2
                    e7 = tag("dve", lambda e, crow=crow, cch=cch, hfq=hfq: e.tensor_tensor(
                        CTXT[crow:crow + 64, cch * T + hfq * 1024:cch * T + hfq * 1024 + 1024],
                        FS[0:64, 0:1024], psB[0:64, :], op=ALU.mult), "DVE")
                    wait("pe", "DVE", e7)
            ctx_done_b = N["DVE"]
        wait("pe", "DVE", ctx_done_b)
        for eo in range(NC):
            wait("gp", "PE", N["PE"])
            wo = tag("gp", lambda e, eo=eo: e.dma_start(
                out=WA[:, 3 * NC * 128: 3 * NC * 128 + NC * 128].rearrange("q (c m) -> q c m", c=NC),
                in_=d_in["Wo"].rearrange("(c q) e -> q c e", q=128)[:, :, eo * 128:(eo + 1) * 128]), "GP", 16)
            wait("pe", "GP", wo)
            for hfq in range(2):
                ps = psA if hfq == 0 else psB
                if res_tok is not None:
                    wait("pe", "DVE", res_tok)
                otok = None
                for c in range(NC):
                    for (s0, s1) in seg512(0, 1024):
                        otok = tag("pe", lambda e, c=c, eo=eo, hfq=hfq, ps=ps, s0=s0, s1=s1: e.matmul(
                            ps[:, s0:s1], WA[:, 3 * NC * 128 + c * 128:3 * NC * 128 + (c + 1) * 128],
                            CTXT[:, c * T + hfq * 1024 + s0:c * T + hfq * 1024 + s1],
                            start=(c == 0), stop=(c == NC - 1)), "PE")
                wait("dve", "PE", otok)
                res_tok = tag("dve", lambda e, eo=eo, hfq=hfq, ps=ps: e.tensor_tensor(
                    XT[:, eo * T + hfq * 1024:eo * T + hfq * 1024 + 1024],
                    XT[:, eo * T + hfq * 1024:eo * T + hfq * 1024 + 1024], ps[:, :], op=ALU.add), "DVE")

        ffn2 = emit_ffn(2)

        wait("sp", "DVE", ffn2)
        sttok = None
        for c in range(NC):
            sttok = tag("sp", lambda e, c=c: e.dma_start(
                out=out_d[c * 128:(c + 1) * 128, :], in_=XT[:, c * T:(c + 1) * T]), "ST", 16)
        wait("sp", "ST", sttok)

        block = es.enter_context(nc.Block())

        @block.sync
        def _(sync):
            for fn in Q["sp"]:
                fn(sync)

        @block.tensor
        def _(tensor):
            for fn in Q["pe"]:
                fn(tensor)

        @block.scalar
        def _(scalar):
            for fn in Q["act"]:
                fn(scalar)

        @block.vector
        def _(vector):
            for fn in Q["dve"]:
                fn(vector)

        @block.gpsimd
        def _(gpsimd):
            for fn in Q["gp"]:
                fn(gpsimd)

    return nc


_NC_CACHE = None


def kernel(**inputs):
    global _NC_CACHE
    x = np.asarray(inputs["L0_attn_a_x"])
    CLa, SLa, CLb, SLb, ID, MSK = _host_consts()
    common = {
        "WQT": np.ascontiguousarray(np.asarray(inputs["L0_attn_a_W_Q"]).T),
        "WKT": np.ascontiguousarray(np.asarray(inputs["L0_attn_a_W_K"]).T),
        "WVT": np.ascontiguousarray(np.asarray(inputs["L0_attn_a_W_V"]).T),
        "WOT": np.ascontiguousarray(np.asarray(inputs["L0_attn_a_W_O"]).T),
        "Wq": np.ascontiguousarray(np.asarray(inputs["L0_attn_b_Wq"])),
        "Wk": np.ascontiguousarray(np.asarray(inputs["L0_attn_b_Wk"])),
        "Wv": np.ascontiguousarray(np.asarray(inputs["L0_attn_b_Wv"])),
        "Wo": np.ascontiguousarray(np.asarray(inputs["L0_attn_b_Wo"])),
        "FWT": np.ascontiguousarray(np.asarray(inputs["L0_ffn_weight"]).T),
        "CLa": CLa, "SLa": SLa, "CLb": CLb, "SLb": SLb, "IDC": ID, "MSK": MSK,
        "lq1": np.asarray(inputs["L0_attn_b_lambda_q1"]).reshape(1, 32).astype(np.float32),
        "lk1": np.asarray(inputs["L0_attn_b_lambda_k1"]).reshape(1, 32).astype(np.float32),
        "lq2": np.asarray(inputs["L0_attn_b_lambda_q2"]).reshape(1, 32).astype(np.float32),
        "lk2": np.asarray(inputs["L0_attn_b_lambda_k2"]).reshape(1, 32).astype(np.float32),
        "rmsw": np.asarray(inputs["L0_attn_b_rms_w"]).reshape(1, 64).astype(np.float32),
        "gam": np.asarray(inputs["L0_ffn_gamma"]).reshape(1, D).astype(np.float32),
        "bet": np.asarray(inputs["L0_ffn_beta"]).reshape(1, D).astype(np.float32),
    }
    in_maps = []
    for c in range(NCORES):
        m = dict(common)
        m["xT"] = np.ascontiguousarray(x[c].T)
        in_maps.append(m)
    if _NC_CACHE is None:
        _NC_CACHE = build()
    res = run_bass_kernel_spmd(_NC_CACHE, in_maps, core_ids=list(range(NCORES)))
    out = np.stack([np.ascontiguousarray(r["outT"].T) for r in res.results], axis=0)
    return out.astype(np.float32)


if __name__ == "__main__":
    import json
    np.random.seed(0)
    pass
